# version 28
# baseline (speedup 1.0000x reference)
import math
from contextlib import ExitStack

import numpy as np
import ml_dtypes

import concourse.bass as bass
import concourse.mybir as mybir
from concourse.bass_utils import run_bass_kernel_spmd

F32 = mybir.dt.float32
BF16 = mybir.dt.bfloat16
AF = mybir.ActivationFunctionType
ALU = mybir.AluOpType
AX = mybir.AxisListType

T = 2048
D = 1024
G = D // 128
NT = T // 128
NC_ = 512
NCH = T // NC_

ENGINES = ("pe", "act", "ve", "gp", "sp")
QFAN = {"qsp": 2, "qgp": 4, "qact": 1}
DMAQ = tuple(f"{q}{i}" for q, n in QFAN.items() for i in range(n))

PHASES = {}


class Buf:
    __slots__ = ("w", "r")

    def __init__(self):
        self.w = None
        self.r = []


class Ctx:
    def __init__(self, nc, stack):
        self.nc = nc
        self.sems = {}
        for s in ENGINES + DMAQ:
            self.sems[s] = stack.enter_context(nc.semaphore("sem_" + s))
        self.cnt = {s: 0 for s in ENGINES + DMAQ}
        self.clock = {e: {s: 0 for s in ENGINES + DMAQ} for e in ENGINES}
        self.ops = {e: [] for e in ENGINES}
        self.phase = "init"
        self.qrr = {q: 0 for q in QFAN}

    def _deps(self, reads, writes, extra):
        deps = {}

        def add(tok):
            if tok is None:
                return
            s, c = tok
            if deps.get(s, 0) < c:
                deps[s] = c

        for b in reads:
            add(b.w)
        for b in writes:
            add(b.w)
            for t in b.r:
                add(t)
        for t in extra:
            add(t)
        return deps

    def op(self, eng, fn, reads=(), writes=(), extra=(), q=None):
        if q is not None:
            i = self.qrr[q]
            self.qrr[q] = (i + 1) % QFAN[q]
            q = f"{q}{i}"
            extra = tuple(extra) + ((q, self.cnt[q]),)
        deps = self._deps(reads, writes, extra)
        waits = {}
        clk = self.clock[eng]
        for s, c in deps.items():
            if clk[s] < c:
                waits[s] = c
                clk[s] = c
        stream = q if q is not None else eng
        inc = 16 if q is not None else 1
        self.cnt[stream] += inc
        tok = (stream, self.cnt[stream])
        self.ops[eng].append((waits, fn, stream, inc, self.phase))
        for b in reads:
            b.r.append(tok)
        for b in writes:
            b.w = tok
            b.r = []
        return tok

    def op_multi(self, eng, fns, reads=(), writes=(), extra=(), q=None):
        return self.op(eng, list(fns), reads=reads, writes=writes, extra=extra,
                       q=q)

    def phase_barrier(self):
        snap = dict(self.cnt)
        for e in ENGINES:
            clk = self.clock[e]
            waits = {}
            for s, c in snap.items():
                if clk[s] < c:
                    waits[s] = c
                    clk[s] = c
            if waits:
                self.ops[e].append((waits, None, None, 0, self.phase))

    def barrier(self, eng, tokens):
        deps = self._deps((), (), tokens)
        clk = self.clock[eng]
        waits = {}
        for s, c in deps.items():
            if clk[s] < c:
                waits[s] = c
                clk[s] = c
        if waits:
            self.ops[eng].append((waits, None, None, 0, self.phase))

    def finalize(self, block):
        engmap = {"pe": block.tensor, "act": block.scalar, "ve": block.vector,
                  "gp": block.gpsimd, "sp": block.sync}
        sems = self.sems
        for e in ENGINES:
            oplist = self.ops[e]
            if not oplist:
                continue

            def body(engine, oplist=oplist):
                cur = None
                sid = None
                for waits, fn, stream, inc, phase in oplist:
                    if phase != cur:
                        if cur is not None:
                            self.nc.leave_named_scope(cur, sid, False)
                        sid, _ = self.nc.enter_named_scope(phase, False)
                        cur = phase
                    for s, c in waits.items():
                        engine.wait_ge(sems[s], c)
                    if fn is None:
                        continue
                    if isinstance(fn, list):
                        for f in fn[:-1]:
                            f(engine)
                        inst = fn[-1](engine)
                    else:
                        inst = fn(engine)
                    inst.then_inc(sems[stream], inc)
                if cur is not None:
                    self.nc.leave_named_scope(cur, sid, False)

            engmap[e](body)


def host_constants():
    c = {}
    c["C_eye_bf"] = np.eye(128, dtype=ml_dtypes.bfloat16)
    c["C_eye_f32"] = np.eye(128, dtype=np.float32)
    ones2 = np.zeros((128, 2), dtype=ml_dtypes.bfloat16)
    ones2[0:64, 0] = 1
    ones2[64:128, 1] = 1
    c["C_ones2"] = ones2
    c["C_ones1"] = np.ones((128, 1), dtype=ml_dtypes.bfloat16)
    c["C_onesrow"] = np.ones((65, 128), dtype=ml_dtypes.bfloat16)
    ind = np.zeros((128, G, 16), dtype=ml_dtypes.bfloat16)
    for g in range(G):
        ind[0:64, g, 2 * g] = 1
        ind[64:128, g, 2 * g + 1] = 1
    c["C_ind"] = ind.reshape(128, G * 16)
    sel2 = np.zeros((16, G, 128), dtype=ml_dtypes.bfloat16)
    for g in range(G):
        sel2[2 * g, g, 0:64] = 1
        sel2[2 * g + 1, g, 64:128] = 1
    c["C_sel2"] = sel2.reshape(16, G * 128)
    sele = np.zeros((8, 8, 128), dtype=ml_dtypes.bfloat16)
    for e in range(8):
        sele[e, e, :] = 1
    c["C_sele"] = sele.reshape(8, 8 * 128)
    dh = 64
    inv = 1.0 / (10000.0 ** (np.arange(0, dh, 2, dtype=np.float64) / dh))
    pos = np.arange(T, dtype=np.float64)
    ang = pos[None, :] * inv[:, None]
    cos32 = np.cos(ang)
    sin32 = np.sin(ang)
    cosf = np.concatenate([cos32, cos32], axis=0)
    sinf = np.concatenate([sin32, sin32], axis=0)
    c["C_cos"] = np.concatenate([cosf, cosf], axis=0).astype(np.float32)
    c["C_sin"] = np.concatenate([sinf, sinf], axis=0).astype(np.float32)
    rot = np.zeros((128, 128), dtype=ml_dtypes.bfloat16)
    for base in (0, 64):
        for j in range(32):
            rot[base + j + 32, base + j] = -1.0
            rot[base + j, base + j + 32] = 1.0
    c["C_rot"] = rot
    c["C_cvals"] = np.tile(np.array([[1e-5, 1e-6, 1.1, 1.0, 0.1, 0.0, 0.5, 2.0]],
                                    dtype=np.float32), (128, 1))
    return c


class State:
    pass


def alloc_psum(nc, stack, S):
    S.pA = stack.enter_context(nc.psum_tensor("pA", [128, 2048], F32))
    S.pB = stack.enter_context(nc.psum_tensor("pB", [128, 2048], F32))
    S.pA_buf = [Buf() for _ in range(4)]
    S.pB_buf = [Buf() for _ in range(4)]

    def bank(i):
        if i < 4:
            return S.pA[:, 512 * i:512 * (i + 1)], S.pA_buf[i]
        return S.pB[:, 512 * (i - 4):512 * (i - 3)], S.pB_buf[i - 4]

    S.bank = bank


def alloc_consts(nc, stack, S):
    ap = lambda h: h[:]
    S.eye_bf = stack.enter_context(nc.sbuf_tensor("eye_bf", [128, 128], BF16))
    S.eye_f32 = stack.enter_context(nc.sbuf_tensor("eye_f32", [128, 128], F32))
    S.ones2 = stack.enter_context(nc.sbuf_tensor("ones2", [128, 2], BF16))
    S.ones1 = stack.enter_context(nc.sbuf_tensor("ones1", [128, 1], BF16))
    S.onesrow = stack.enter_context(nc.sbuf_tensor("onesrow", [65, 128], BF16))
    S.ind = stack.enter_context(nc.sbuf_tensor("ind", [128, G * 16], BF16))
    S.sel2 = stack.enter_context(nc.sbuf_tensor("sel2", [16, G * 128], BF16))
    S.rows = stack.enter_context(nc.sbuf_tensor("rows", [1, T], F32))
    S.rows_buf = Buf()
    S.rowsb = stack.enter_context(nc.sbuf_tensor("rowsb", [1, 3 * T], BF16))
    S.rowsb_buf = Buf()
    S.rot = stack.enter_context(nc.sbuf_tensor("rot", [128, 128], BF16))
    S.cvals = stack.enter_context(nc.sbuf_tensor("cvals", [128, 8], F32))
    CVALS = [1e-5, 1e-6, 1.1, 1.0, 0.1, 0.0, 0.5, 2.0]
    for i, v in enumerate(CVALS):
        nc.const_aps.register(F32, v, S.cvals[:, i:i + 1])
    S.const_buf = Buf()
    names = ["C_eye_bf", "C_eye_f32", "C_ones2", "C_ones1", "C_onesrow",
             "C_ind", "C_sel2", "C_rot", "C_cvals"]
    handles = [S.eye_bf, S.eye_f32, S.ones2, S.ones1, S.onesrow, S.ind,
               S.sel2, S.rot, S.cvals]
    for name, h in zip(names, handles):
        d = S.dram[name]
        S.ctx.op("gp", lambda e, h=h, d=d: e.dma_start(out=h[:], in_=d),
                 writes=(S.const_buf,), q="qgp")


def emit_load_xT_from_x(S):
    ctx, nc = S.ctx, S.nc
    xrows = S.stack_phase.enter_context(nc.sbuf_tensor("p0_xrows", [128, NT * D], BF16))
    xrows_buf = Buf()
    src = S.dram["x"].rearrange("(t p) d -> p t d", p=128)
    dst = xrows[:].rearrange("p (t d) -> p t d", d=D)
    ctx.op("gp", lambda e: e.dma_start(out=dst, in_=src),
           writes=(xrows_buf,), q="qgp")
    for t in range(NT):
        for g in range(G):
            bk, bkbuf = S.bank((t * G + g) % 8)
            bkb = bk.bitcast(BF16)
            ctx.op("pe", lambda e, t=t, g=g, bkb=bkb: e.transpose(
                bkb[:, 0:128], xrows[:, t * D + g * 128: t * D + (g + 1) * 128],
                S.eye_bf[:]),
                reads=(xrows_buf, S.const_buf), writes=(bkbuf,))
            ctx.op("ve", lambda e, t=t, g=g, bkb=bkb: e.tensor_copy(
                S.xT[:, g, t * 128:(t + 1) * 128], bkb[:, 0:128]),
                reads=(bkbuf,), writes=(S.xT_buf[g],))


def emit_load_xT_handoff(S):
    ctx = S.ctx
    src = S.dram["xT_in"]
    for g in range(G):
        ctx.op("gp", lambda e, g=g: e.dma_start(
            out=S.xT[:, g, :], in_=src[:, g * T:(g + 1) * T]),
            writes=(S.xT_buf[g],), q="qgp")


def emit_store_handoff(S):
    ctx = S.ctx
    dst = S.dram["xT_out"]
    toks = []
    for g in range(G):
        toks.append(ctx.op("sp", lambda e, g=g: e.dma_start(
            out=dst[:, g * T:(g + 1) * T], in_=S.xT[:, g, :]),
            reads=(S.xT_buf[g],), q="qsp"))
    ctx.barrier("sp", toks)


def emit_store_out_from_f32(S, x4, x4_buf):
    ctx, nc = S.ctx, S.nc
    rowbuf = [S.stack_phase.enter_context(nc.sbuf_tensor(f"p5_row{i}", [128, D], F32))
              for i in range(2)]
    rowbuf_buf = [Buf(), Buf()]
    out = S.dram["out"]
    toks = []
    for t in range(NT):
        rb = rowbuf[t % 2]
        rbb = rowbuf_buf[t % 2]
        for g in range(G):
            bk, bkbuf = S.bank((t * G + g) % 8)
            ctx.op("pe", lambda e, t=t, g=g, bk=bk: e.transpose(
                bk[:, 0:128], x4[:, g, t * 128:(t + 1) * 128], S.eye_f32[:]),
                reads=(x4_buf[g], S.const_buf), writes=(bkbuf,))
            ctx.op("ve", lambda e, g=g, bk=bk, rb=rb: e.tensor_copy(
                rb[:, g * 128:(g + 1) * 128], bk[:, 0:128]),
                reads=(bkbuf,), writes=(rbb,))
        eng, qq = ("sp", "qsp") if t % 2 == 0 else ("gp", "qgp")
        toks.append(ctx.op(eng, lambda e, t=t, rb=rb: e.dma_start(
            out=out[t * 128:(t + 1) * 128, :], in_=rb[:]),
            reads=(rbb,), q=qq))
    ctx.barrier("sp", toks)


def load_weight_2d(S, name, nkt, nmt, mt_w=128, tag=None):
    nc = S.nc
    tag = tag or name
    h = S.stack_phase.enter_context(
        nc.sbuf_tensor("w_" + tag, [128, nkt, nmt * mt_w], BF16))
    buf = Buf()
    src = S.dram[name].rearrange("(kt p) m -> p kt m", p=128)
    S.ctx.op("gp", lambda e: e.dma_start(out=h[:], in_=src), writes=(buf,), q="qgp")
    return h, buf


def emit_ln(S, r, r_buf, dst, dst_buf, scratch, scratch_buf, out_f32=False):
    ctx, nc = S.ctx, S.nc
    allbanks = [S.bank(i)[1] for i in range(8)]
    for g in range(G):
        ctx.op("ve", lambda e, g=g: e.tensor_tensor(
            scratch[:], r[:, g, :], r[:, g, :], ALU.mult),
            reads=(r_buf[g],), writes=(scratch_buf,))
        fns = []
        for c in range(NCH):
            fns.append(lambda e, g=g, c=c: e.matmul(
                S.pA[0:1, c * 512:(c + 1) * 512], S.ones1[:],
                r[:, g, c * 512:(c + 1) * 512],
                start=(g == 0), stop=(g == G - 1)))
        for c in range(NCH):
            fns.append(lambda e, g=g, c=c: e.matmul(
                S.pB[0:1, c * 512:(c + 1) * 512], S.ones1[:],
                scratch[:, c * 512:(c + 1) * 512],
                start=(g == 0), stop=(g == G - 1)))
        ctx.op_multi("pe", fns, reads=(r_buf[g], scratch_buf, S.const_buf),
                     writes=tuple(allbanks))
    rows_buf = S.rows_buf
    rstdb = S.rowsb_buf

    def rows0(sl=None):
        a = sl.start if sl else 0
        b = sl.stop if sl else T
        return S.rows[0:1, a:b]

    def rowsb(i, sl=None):
        a = i * T + (sl.start if sl else 0)
        b = i * T + (sl.stop if sl else T)
        return S.rowsb[0:1, a:b]

    pa_bufs = [S.bank(c)[1] for c in range(4)]
    pb_bufs = [S.bank(4 + c)[1] for c in range(4)]
    ctx.op("ve", lambda e: e.tensor_scalar(rowsb(0), S.pA[0:1, 0:T], 1.0 / D,
                                           None, ALU.mult, ALU.bypass),
           reads=tuple(pa_bufs), writes=(rstdb,))
    ctx.op("ve", lambda e: e.tensor_copy(rows0(), S.pB[0:1, 0:T]),
           reads=tuple(pb_bufs), writes=(rows_buf,))
    ctx.op("act", lambda e: e.activation(S.pA[0:1, 0:T], rowsb(0), AF.Square),
           reads=(rstdb,), writes=tuple(pa_bufs))
    ctx.op("ve", lambda e: e.scalar_tensor_tensor(
        rows0(), rows0(), 1.0 / D, S.pA[0:1, 0:T], ALU.mult, ALU.subtract),
        reads=(rows_buf,) + tuple(pa_bufs), writes=(rows_buf,))
    ctx.op("act", lambda e: e.activation(rows0(), rows0(), AF.Ln, bias=1e-5),
           reads=(rows_buf, S.const_buf), writes=(rows_buf,))
    ctx.op("act", lambda e: e.activation(rowsb(1), rows0(), AF.Exp, scale=-0.5),
           reads=(rows_buf,), writes=(rstdb,))
    ctx.op("ve", lambda e: e.scalar_tensor_tensor(
        rowsb(2), rowsb(0), -1.0, rowsb(1), ALU.mult, ALU.mult),
        reads=(rstdb,), writes=(rstdb,))
    fns = []
    for c in range(NCH):
        fns.append(lambda e, c=c: e.matmul(
            S.pA[:, c * 512:(c + 1) * 512], S.onesrow[0:1, :],
            rowsb(1, slice(c * 512, (c + 1) * 512)), start=True, stop=True))
    for c in range(NCH):
        fns.append(lambda e, c=c: e.matmul(
            S.pB[:, c * 512:(c + 1) * 512], S.onesrow[0:1, :],
            rowsb(2, slice(c * 512, (c + 1) * 512)), start=True, stop=True))
    ctx.op_multi("pe", fns, reads=(rstdb, S.const_buf), writes=tuple(allbanks))
    pa_bufs = [S.bank(c)[1] for c in range(4)]
    pb_bufs = [S.bank(4 + c)[1] for c in range(4)]
    for g in range(G):
        ctx.op("ve", lambda e, g=g: e.tensor_tensor(
            scratch[:], r[:, g, :], S.pA[:, 0:T], ALU.mult),
            reads=(r_buf[g],) + tuple(pa_bufs), writes=(scratch_buf,))
        ctx.op("ve", lambda e, g=g: e.tensor_tensor(
            dst[:, g, :], scratch[:], S.pB[:, 0:T], ALU.add),
            reads=(scratch_buf,) + tuple(pb_bufs), writes=(dst_buf[g],))


def emit_l0a(S):
    ctx, nc = S.ctx, S.nc
    ph = S.stack_phase
    wq, wq_b = load_weight_2d(S, "L0_attn_Wq", 8, 8)
    wk, wk_b = load_weight_2d(S, "L0_attn_Wk", 8, 2)
    wv, wv_b = load_weight_2d(S, "L0_attn_Wv", 8, 2)
    wo, wo_b = wq, wq_b

    QT = ph.enter_context(nc.sbuf_tensor("l0_QT", [128, 8, T], BF16))
    QT_b = [Buf() for _ in range(8)]
    KTD = ph.enter_context(nc.sbuf_tensor("l0_KTD", [128, 4, T], BF16))
    KTD_b = [Buf() for _ in range(4)]
    Vg = ph.enter_context(nc.sbuf_tensor("l0_Vg", [128, NT, 4 * 65], BF16))
    Vg_b = Buf()
    AO = ph.enter_context(nc.sbuf_tensor("l0_AO", [128, 8, T], BF16))
    AO_b = [Buf() for _ in range(8)]
    cos_ap = AO[:, 0:2, :].bitcast(F32).rearrange("p a b -> p (a b)")
    sin_ap = AO[:, 2:4, :].bitcast(F32).rearrange("p a b -> p (a b)")
    tab_b = Buf()
    ctx.op("gp", lambda e: e.dma_start(out=cos_ap, in_=S.dram["C_cos"]),
           writes=(tab_b,), q="qgp")
    ctx.op("gp", lambda e: e.dma_start(out=sin_ap, in_=S.dram["C_sin"]),
           writes=(tab_b,), q="qgp")
    expS = [ph.enter_context(nc.sbuf_tensor(f"l0_expS{i}", [128, 1024], BF16))
            for i in range(2)]
    expS_b = [Buf(), Buf()]
    expS2 = [ph.enter_context(nc.sbuf_tensor(f"l0_expT{i}", [128, 1024], BF16))
             for i in range(2)]
    expS2_b = [Buf(), Buf()]
    scr = [ph.enter_context(nc.sbuf_tensor("l0_scr0", [128, T], BF16))]
    scr_b = [Buf()]
    nrm = ph.enter_context(nc.sbuf_tensor("l0_nrm", [65, 1024], BF16))
    nrm_b = Buf()
    rec_row = nrm[64:65, :]
    rec_b = Buf()
    sumsq = ph.enter_context(nc.sbuf_tensor("l0_ssq", [16, T], BF16))
    sumsq_b = Buf()
    tiny = ph.enter_context(nc.sbuf_tensor("l0_tiny", [32, 16], F32))
    tiny_b = Buf()
    bias_sb = ph.enter_context(nc.sbuf_tensor("l0_bias", [128, 1], F32))
    bias_b = Buf()

    def proj_group(w, w_buf, mt, pp, rhs=None, rhs_bufs=None):
        rhs = rhs if rhs is not None else S.xT
        rhs_bufs = rhs_bufs if rhs_bufs is not None else S.xT_buf
        fns = []
        for kt in range(G):
            for c in range(NCH):
                fns.append(lambda e, kt=kt, c=c, w=w, mt=mt, pp=pp: e.matmul(
                    pp[:, c * 512:(c + 1) * 512],
                    w[:, kt, mt * 128:(mt + 1) * 128],
                    rhs[:, kt, c * 512:(c + 1) * 512],
                    start=(kt == 0), stop=(kt == G - 1)))
        pbufs = tuple(S.bank((0 if pp is S.pA else 4) + c)[1] for c in range(4))
        tok = ctx.op_multi("pe", fns,
                           reads=(w_buf,) + tuple(rhs_bufs) + (S.const_buf,),
                           writes=pbufs)
        return tok, pbufs

    def std_proj(w, w_buf, nmt, dst, dst_b):
        for mt in range(nmt):
            pp = S.pA if mt % 2 == 0 else S.pB
            tok, pbufs = proj_group(w, w_buf, mt, pp)
            ctx.op("ve", lambda e, dst=dst, mt=mt, pp=pp: e.tensor_copy(
                dst[:, mt, :], pp[:, 0:T]),
                reads=pbufs, writes=(dst_b[mt],))

    std_proj(wq, wq_b, 8, QT, QT_b)
    for mt in range(2):
        pp = S.pA if mt % 2 == 0 else S.pB
        tok, pbufs = proj_group(wk, wk_b, mt, pp)
        for hh in range(2):
            for half in range(2):
                ctx.op("ve", lambda e, mt=mt, hh=hh, half=half, pp=pp:
                       e.tensor_copy(
                    KTD[half * 64:half * 64 + 64, 2 * mt + hh, :],
                    pp[hh * 64:hh * 64 + 64, 0:T]),
                    reads=pbufs, writes=(KTD_b[2 * mt + hh],))

    ones_col = Vg[:].rearrange("p s (h c) -> p s h c", c=65)[:, :, :, 64:65]
    ctx.op("ve", lambda e: e.memset(ones_col, 1.0), writes=(Vg_b,))
    for st in range(NT):
        bi = 4 + st % 2
        bk, bkbuf = S.bank(bi)
        fns = [lambda e, st=st, kt=kt, bk=bk: e.matmul(
            bk[:, 0:256], S.xT[:, kt, st * 128:(st + 1) * 128],
            wv[:, kt, :], start=(kt == 0), stop=(kt == G - 1))
            for kt in range(G)]
        ctx.op_multi("pe", fns, reads=(wv_b,) + tuple(S.xT_buf),
                     writes=(bkbuf,))
        vgv = Vg[:].rearrange("p s (h c) -> p s h c", c=65)
        ctx.op("ve", lambda e, st=st, bk=bk, vgv=vgv: e.tensor_copy(
            vgv[:, st, :, 0:64],
            bk[:, 0:256].rearrange("p (h c) -> p h c", c=64)),
            reads=(bkbuf,), writes=(Vg_b,))

    def rope(dst, dst_b, g):
        pbufs = [S.bank(c)[1] for c in range(NCH)]
        fns = [lambda e, c=c: e.matmul(
            S.pA[:, c * 512:(c + 1) * 512], S.rot[:],
            dst[:, g, c * 512:(c + 1) * 512], start=True, stop=True)
            for c in range(NCH)]
        ctx.op_multi("pe", fns, reads=(dst_b[g], S.const_buf),
                     writes=tuple(pbufs))
        m1, m1_b = scr[0], scr_b[0]
        ctx.op("ve", lambda e: e.tensor_tensor(m1[:], dst[:, g, :], cos_ap,
                                               ALU.mult),
               reads=(dst_b[g], tab_b), writes=(m1_b,))
        ctx.op("ve", lambda e: e.tensor_tensor(dst[:, g, :], S.pA[:, 0:T],
                                               sin_ap, ALU.mult),
               reads=tuple(pbufs) + (tab_b,), writes=(dst_b[g],))
        ctx.op("ve", lambda e: e.tensor_tensor(dst[:, g, :], m1[:], dst[:, g, :],
                                               ALU.add),
               reads=(m1_b,), writes=(dst_b[g],))

    for g in range(8):
        rope(QT, QT_b, g)
    for g in range(4):
        rope(KTD, KTD_b, g)

    def head_sumsq(src, src_b, nptile, dst_row):
        for g in range(nptile):
            ctx.op("ve", lambda e, g=g: e.tensor_tensor(
                scr[0][:], src[:, g, :], src[:, g, :], ALU.mult),
                reads=(src_b[g],), writes=(scr_b[0],))
            fns = [lambda e, g=g, c=c: e.matmul(
                S.pA[0:16, c * 512:(c + 1) * 512],
                S.ind[:, g * 16:(g + 1) * 16],
                scr[0][:, c * 512:(c + 1) * 512],
                start=(g == 0), stop=(g == nptile - 1)) for c in range(NCH)]
            ctx.op_multi("pe", fns, reads=(scr_b[0], S.const_buf),
                         writes=tuple(S.bank(c)[1] for c in range(4)))
        pa_bufs = [S.bank(c)[1] for c in range(4)]
        ctx.op("ve", lambda e: e.tensor_copy(sumsq[:, :], S.pA[0:16, 0:T]),
               reads=tuple(pa_bufs), writes=(sumsq_b,))
        ctx.op("ve", lambda e: e.tensor_reduce(
            tiny[0:16, 0:1], sumsq[:, :], AX.X, ALU.max),
            reads=(sumsq_b,), writes=(tiny_b,))
        bk, bkbuf = S.bank(5)
        ctx.op("pe", lambda e, bk=bk: e.transpose(bk[0:1, 0:16], tiny[0:16, 0:1],
                                                  S.eye_f32[0:16, 0:16]),
               reads=(tiny_b, S.const_buf), writes=(bkbuf,))
        ctx.op("ve", lambda e: e.tensor_copy(tiny[0:1, 0:16], bk[0:1, 0:16]),
               reads=(bkbuf,), writes=(tiny_b,))
        ctx.op("ve", lambda e, dst_row=dst_row: e.tensor_reduce(
            tiny[0:1, dst_row:dst_row + 1], tiny[0:1, 0:16], AX.X, ALU.max),
            reads=(tiny_b,), writes=(tiny_b,))

    src_wo = S.dram["L0_attn_Wo"].rearrange("(kt p) m -> p kt m", p=128)
    ctx.op("gp", lambda e: e.dma_start(out=wo[:], in_=src_wo), writes=(wo_b,), q="qgp")

    head_sumsq(QT, QT_b, 8, 4)
    head_sumsq(KTD, KTD_b, 4, 5)
    ctx.op("ve", lambda e: e.tensor_tensor(
        tiny[0:1, 6:7], tiny[0:1, 4:5], tiny[0:1, 5:6], ALU.mult),
        reads=(tiny_b,), writes=(tiny_b,))
    ctx.op("act", lambda e: e.activation(tiny[0:1, 6:7], tiny[0:1, 6:7], AF.Ln),
           reads=(tiny_b,), writes=(tiny_b,))
    ctx.op("act", lambda e: e.activation(tiny[0:1, 7:8], tiny[0:1, 6:7], AF.Exp,
                                         scale=0.5),
           reads=(tiny_b,), writes=(tiny_b,))
    ctx.op("ve", lambda e: e.tensor_scalar(
        rec_row[0:1, 0:1], tiny[0:1, 7:8], -0.125, None, ALU.mult, ALU.bypass),
        reads=(tiny_b,), writes=(rec_b,))
    bk6, bk6buf = S.bank(6)
    ctx.op("pe", lambda e: e.matmul(bk6[:, 0:1], S.onesrow[64:65, :], rec_row[0:1, 0:1],
                                    start=True, stop=True),
           reads=(rec_b, S.const_buf), writes=(bk6buf,))
    ctx.op("ve", lambda e: e.tensor_copy(bias_sb[:], bk6[:, 0:1]),
           reads=(bk6buf,), writes=(bias_b,))

    expS4 = [expS[0], expS[1], expS2[0], expS2[1]]
    expS4_b = [expS_b[0], expS_b[1], expS2_b[0], expS2_b[1]]
    for h in range(16):
        g = h // 4
        qp, qr = h // 2, (h % 2) * 64

        def dot_group(a_, st):
            d0 = a_ * 2
            q0 = a_ * 1024
            dbufs = (S.bank(d0)[1], S.bank(d0 + 1)[1])
            fns = [lambda e, j=j, st=st, d0=d0, q0=q0: e.matmul(
                S.pA[:, (d0 + j) * 512:(d0 + j + 1) * 512],
                KTD[qr:qr + 64, g, st * 128:(st + 1) * 128],
                QT[qr:qr + 64, qp, q0 + j * 512:q0 + (j + 1) * 512],
                start=True, stop=True) for j in range(2)]
            ctx.op_multi("pe", fns, reads=(KTD_b[g], QT_b[qp]), writes=dbufs)
            return dbufs

        def av_group(a_, st):
            es, es_b = expS4[a_ * 2 + st % 2], expS4_b[a_ * 2 + st % 2]
            fns = [lambda e, j=j, st=st, es=es, a_=a_: e.matmul(
                S.pB[0:65, (a_ * 2 + j) * 512:(a_ * 2 + j + 1) * 512],
                Vg[:, st, g * 65:(g + 1) * 65],
                es[:, j * 512:(j + 1) * 512],
                start=(st == 0), stop=(st == NT - 1)) for j in range(2)]
            ctx.op_multi("pe", fns, reads=(es_b, Vg_b),
                         writes=(S.bank(4 + a_ * 2)[1], S.bank(5 + a_ * 2)[1]))

        def exp_step(a_, st, dbufs):
            d0 = a_ * 2
            dot = S.pA[:, d0 * 512:(d0 + 2) * 512]
            es, es_b = expS4[a_ * 2 + st % 2], expS4_b[a_ * 2 + st % 2]
            ctx.op("act", lambda e, es=es, dot=dot: e.activation(
                es[:], dot, AF.Exp, bias=bias_sb[:], scale=0.125),
                reads=tuple(dbufs) + (bias_b,), writes=(es_b,))

        for st in range(NT):
            dbA = dot_group(0, st)
            dbB = dot_group(1, st)
            exp_step(0, st, dbA)
            if st > 0:
                av_group(0, st - 1)
                av_group(1, st - 1)
            exp_step(1, st, dbB)
        av_group(0, NT - 1)
        av_group(1, NT - 1)

        for a_ in range(2):
            av_bufs = (S.bank(4 + a_ * 2)[1], S.bank(5 + a_ * 2)[1])
            q0 = a_ * 1024
            ctx.op("act", lambda e, a_=a_: e.activation(
                rec_row[0:1, :], S.pB[64:65, a_ * 1024:(a_ + 1) * 1024], AF.Ln),
                reads=av_bufs, writes=(rec_b,))
            ctx.op("act", lambda e: e.activation(
                rec_row[0:1, :], rec_row[0:1, :], AF.Exp, scale=-1.0),
                reads=(rec_b,), writes=(rec_b,))
            bcb = (S.bank(2 * a_)[1], S.bank(2 * a_ + 1)[1])
            ctx.op_multi("pe", [lambda e, j=j, a_=a_: e.matmul(
                S.pA[0:64, (2 * a_ + j) * 512:(2 * a_ + j + 1) * 512],
                S.onesrow[64:65, 0:64],
                rec_row[0:1, j * 512:(j + 1) * 512], start=True, stop=True)
                for j in range(2)],
                reads=(rec_b, S.const_buf), writes=bcb)
            ctx.op("ve", lambda e, a_=a_: e.tensor_copy(
                nrm[0:64, :], S.pA[0:64, 2 * a_ * 512:(2 * a_ + 2) * 512]),
                reads=bcb, writes=(nrm_b,))
            wr = (AO_b[qp], tab_b) if qp < 4 else (AO_b[qp],)
            ctx.op("ve", lambda e, q0=q0, qr=qr, qp=qp, a_=a_: e.tensor_tensor(
                AO[qr:qr + 64, qp, q0:q0 + 1024],
                S.pB[0:64, a_ * 1024:(a_ + 1) * 1024],
                nrm[0:64, :], ALU.mult),
                reads=(nrm_b,) + av_bufs, writes=wr)

    r, r_b = QT, QT_b
    for mt in range(8):
        pp = S.pA if mt % 2 == 0 else S.pB
        tok, pbufs = proj_group(wo, wo_b, mt, pp, rhs=AO, rhs_bufs=AO_b)
        ctx.op("ve", lambda e, mt=mt, pp=pp: e.tensor_tensor(
            r[:, mt, :], pp[:, 0:T], S.xT[:, mt, :], ALU.add),
            reads=pbufs + (S.xT_buf[mt],), writes=(r_b[mt],))

    emit_ln(S, r, r_b, S.xT, S.xT_buf, scr[0], scr_b[0])


PHASES["l0a"] = emit_l0a


def emit_router(S, wg_name, bg_name, temp, g8, g8_b, sigscale):
    ctx, nc = S.ctx, S.nc
    ph = S.stack_phase
    tg = wg_name
    wg, wg_b = load_weight_2d(S, wg_name, 8, 1, mt_w=8, tag=wg_name + "_r")
    lg = ph.enter_context(nc.sbuf_tensor(tg + "rt_lg", [128, 128], F32))
    lg_b = Buf()
    t1 = ph.enter_context(nc.sbuf_tensor(tg + "rt_t1", [128, 128], F32))
    t1_b = Buf()
    t2 = ph.enter_context(nc.sbuf_tensor(tg + "rt_t2", [128, 128], F32))
    t2_b = Buf()
    mm_ = ph.enter_context(nc.sbuf_tensor(tg + "rt_mm", [128, 64], F32))
    mm_b = Buf()
    fns = []
    for kt in range(G):
        for c in range(NCH):
            fns.append(lambda e, c=c, kt=kt: e.matmul(
                S.pA[0:8, c * 512:(c + 1) * 512], wg[:, kt, :],
                S.xT[:, kt, c * 512:(c + 1) * 512],
                start=(kt == 0), stop=(kt == G - 1)))
    ctx.op_multi("pe", fns, reads=(wg_b,) + tuple(S.xT_buf) + (S.const_buf,),
                 writes=tuple(S.bank(c)[1] for c in range(4)))
    lgT = ph.enter_context(nc.sbuf_tensor(tg + "rt_lgT", [8, T], F32))
    lgT_b = Buf()
    pa_bufs0 = [S.bank(c)[1] for c in range(4)]
    ctx.op("ve", lambda e: e.tensor_copy(lgT[:], S.pA[0:8, 0:T]),
           reads=tuple(pa_bufs0), writes=(lgT_b,))
    if bg_name is not None:
        bg = ph.enter_context(nc.sbuf_tensor(tg + "rt_bg", [8, 1], F32))
        bg_b = Buf()
        ctx.op("gp", lambda e: e.dma_start(out=bg[:], in_=S.dram[bg_name].unsqueeze(1)),
               writes=(bg_b,), q="qgp")
        ctx.op("ve", lambda e: e.tensor_tensor(
            lgT[:], lgT[:], bg[:].broadcast_to([8, T]), ALU.add),
            reads=(bg_b,), writes=(lgT_b,))
    b4, b4buf = S.bank(4)
    ctx.op_multi("pe", [lambda e, t=t, b4=b4: e.transpose(
        b4[:, 8 * t:8 * t + 8], lgT[:, t * 128:(t + 1) * 128],
        S.eye_f32[0:8, 0:8]) for t in range(NT)],
        reads=(lgT_b, S.const_buf), writes=(b4buf,))
    ctx.op("ve", lambda e, b4=b4: e.tensor_copy(lg[:], b4[:, 0:128]),
           reads=(b4buf,), writes=(lg_b,))
    lgv = lg[:].rearrange("p (t e) -> p t e", e=8)
    m1 = mm_[:, 0:16]
    m2 = mm_[:, 16:32]
    w2 = mm_[:, 32:48]
    w1 = mm_[:, 48:64]
    m1b = m1.unsqueeze(2).broadcast_to([128, 16, 8])
    m2b = m2.unsqueeze(2).broadcast_to([128, 16, 8])
    w1b = w1.unsqueeze(2).broadcast_to([128, 16, 8])
    w2b = w2.unsqueeze(2).broadcast_to([128, 16, 8])
    t1v = t1[:].rearrange("p (t e) -> p t e", e=8)
    t2v = t2[:].rearrange("p (t e) -> p t e", e=8)
    ctx.op("ve", lambda e: e.tensor_reduce(m1, lgv, AX.X, ALU.max),
           reads=(lg_b,), writes=(mm_b,))
    ctx.op("ve", lambda e: e.tensor_tensor(t1v, lgv, m1b, ALU.is_equal),
           reads=(lg_b, mm_b), writes=(t1_b,))
    ctx.op("ve", lambda e: e.scalar_tensor_tensor(
        t2v, t1v, -1e30, lgv, ALU.mult, ALU.add),
        reads=(t1_b, lg_b), writes=(t2_b,))
    ctx.op("ve", lambda e: e.tensor_reduce(m2, t2v, AX.X, ALU.max),
           reads=(t2_b,), writes=(mm_b,))
    ctx.op("ve", lambda e: e.tensor_tensor(mm_[:, 32:48], m2, m1, ALU.subtract),
           reads=(mm_b,), writes=(mm_b,))
    ctx.op("act", lambda e: e.activation(w2, mm_[:, 32:48], AF.Sigmoid,
                                         scale=sigscale),
           reads=(mm_b,), writes=(mm_b,))
    ctx.op("ve", lambda e: e.tensor_scalar(w1, w2, -1.0, 1.0, ALU.mult, ALU.add),
           reads=(mm_b,), writes=(mm_b,))
    ctx.op("ve", lambda e: e.tensor_tensor(t2v, lgv, m2b, ALU.is_equal),
           reads=(lg_b, mm_b), writes=(t2_b,))
    ctx.op("ve", lambda e: e.tensor_tensor(t1v, t1v, w1b, ALU.mult),
           reads=(t1_b, mm_b), writes=(t1_b,))
    ctx.op("ve", lambda e: e.tensor_tensor(t2v, t2v, w2b, ALU.mult),
           reads=(t2_b, mm_b), writes=(t2_b,))
    ctx.op("ve", lambda e: e.tensor_tensor(t1[:], t1[:], t2[:], ALU.add),
           reads=(t1_b, t2_b), writes=(t1_b,))
    ctx.op_multi("pe", [lambda e, t=t: e.transpose(
        S.pA[0:8, t * 128:(t + 1) * 128], t1[:, 8 * t:8 * t + 8],
        S.eye_f32[:]) for t in range(NT)],
        reads=(t1_b, S.const_buf),
        writes=tuple(S.bank(c)[1] for c in range(4)))
    pa_bufs = [S.bank(c)[1] for c in range(4)]
    ctx.op("ve", lambda e: e.tensor_copy(g8[0:8, :], S.pA[0:8, 0:T]),
           reads=tuple(pa_bufs), writes=(g8_b,))


def emit_l0f(S):
    ctx, nc = S.ctx, S.nc
    ph = S.stack_phase
    g8 = ph.enter_context(nc.sbuf_tensor("f0_g8", [8, T], BF16))
    g8_b = Buf()
    S.sele = ph.enter_context(nc.sbuf_tensor("sele_l", [8, 8 * 128], BF16))
    ctx.op("gp", lambda e: e.dma_start(out=S.sele[:], in_=S.dram["C_sele"]),
           writes=(S.const_buf,), q="qgp")
    emit_router(S, "L0_ffn_gate_w", None, 1.0, g8, g8_b, 1.0)

    yacc = ph.enter_context(nc.sbuf_tensor("f0_yacc", [128, G, T], BF16))
    yacc_b = [Buf() for _ in range(G)]
    hid = ph.enter_context(nc.sbuf_tensor("f0_hid", [128, 8, T], BF16))
    hid_b = [Buf() for _ in range(8)]
    wdn = ph.enter_context(nc.sbuf_tensor("f0_wdn", [128, 16, 1024], BF16))
    wdn_b = Buf()
    NSLOT = 4
    wgu = [ph.enter_context(nc.sbuf_tensor(f"f0_wgu{i}", [128, 8, 256], BF16))
           for i in range(NSLOT)]
    wgu_b = [Buf() for _ in range(NSLOT)]
    gbc = [ph.enter_context(nc.sbuf_tensor(f"f0_gbc{i}", [128, T], BF16))
           for i in range(2)]
    gbc_b = [Buf(), Buf()]
    tsc = ph.enter_context(nc.sbuf_tensor("f0_tsc", [128, T], BF16))
    tsc_b = Buf()

    gup = S.dram["L0_ffn_gate_up_w"]
    dwn = S.dram["L0_ffn_down_w"]
    FF = 2048

    def dma_pair(e_i, i_g):
        slot = i_g % NSLOT
        w, w_b = wgu[slot], wgu_b[slot]
        ctx.op("gp", lambda e, e_i=e_i, i_g=i_g, w=w: e.dma_start(
            out=w[:, :, 0:128],
            in_=gup[e_i, :, i_g * 128:(i_g + 1) * 128]
            .rearrange("(kt p) m -> p kt m", p=128)),
            writes=(w_b,), q="qgp")
        ctx.op("gp", lambda e, e_i=e_i, i_g=i_g, w=w: e.dma_start(
            out=w[:, :, 128:256],
            in_=gup[e_i, :, FF + i_g * 128:FF + (i_g + 1) * 128]
            .rearrange("(kt p) m -> p kt m", p=128)),
            writes=(w_b,), q="qgp")

    for e_i in range(8):
        dma_pair(e_i, 0)
        dma_pair(e_i, 1)
        ctx.op("gp", lambda e, e_i=e_i: e.dma_start(
            out=wdn[:], in_=dwn[e_i].rearrange("(kt p) m -> p kt m", p=128)),
            writes=(wdn_b,), q="qgp")
        gb, gb_buf = gbc[e_i % 2], gbc_b[e_i % 2]
        bk, bkbuf = S.bank(4)
        ctx.op_multi("pe", [lambda e, e_i=e_i, c=c: e.matmul(
            S.pB[:, c * 512:(c + 1) * 512],
            S.sele[:, e_i * 128:(e_i + 1) * 128],
            g8[0:8, c * 512:(c + 1) * 512], start=True, stop=True)
            for c in range(NCH)],
            reads=(g8_b, S.const_buf),
            writes=tuple(S.bank(4 + c)[1] for c in range(4)))
        ctx.op("ve", lambda e, gb=gb: e.tensor_copy(gb[:], S.pB[:, 0:T]),
               reads=tuple(S.bank(4 + c)[1] for c in range(4)),
               writes=(gb_buf,))
        for s in range(2):
            for j in range(8):
                i_g = s * 8 + j
                slot = (s * 8 + j) % NSLOT
                w, w_b = wgu[slot], wgu_b[slot]
                if i_g >= 2:
                    dma_pair(e_i, i_g)
                pa_bufs = tuple(S.bank(c)[1] for c in range(4))
                pb_bufs = tuple(S.bank(4 + c)[1] for c in range(4))
                fns = []
                for kt in range(G):
                    for c in range(NCH):
                        fns.append(lambda e, c=c, kt=kt, w=w: e.matmul(
                            S.pA[:, c * 512:(c + 1) * 512], w[:, kt, 0:128],
                            S.xT[:, kt, c * 512:(c + 1) * 512],
                            start=(kt == 0), stop=(kt == G - 1)))
                ctx.op_multi("pe", fns, reads=(w_b,) + tuple(S.xT_buf),
                             writes=pa_bufs)
                fns = []
                for kt in range(G):
                    for c in range(NCH):
                        fns.append(lambda e, c=c, kt=kt, w=w: e.matmul(
                            S.pB[:, c * 512:(c + 1) * 512], w[:, kt, 128:256],
                            S.xT[:, kt, c * 512:(c + 1) * 512],
                            start=(kt == 0), stop=(kt == G - 1)))
                ctx.op_multi("pe", fns, reads=(w_b,) + tuple(S.xT_buf),
                             writes=pb_bufs)
                ctx.op("act", lambda e: e.activation(tsc[:], S.pA[:, 0:T],
                                                     AF.Silu),
                       reads=pa_bufs, writes=(tsc_b,))
                ctx.op("ve", lambda e, j=j, gb=gb: e.tensor_tensor(
                    hid[:, j, :], S.pB[:, 0:T], gb[:], ALU.mult),
                    reads=pb_bufs + (gb_buf,), writes=(hid_b[j],))
                ctx.op("ve", lambda e, j=j: e.tensor_tensor(
                    hid[:, j, :], hid[:, j, :], tsc[:], ALU.mult),
                    reads=(tsc_b,), writes=(hid_b[j],))
            for m in range(G):
                pp = S.pA if m % 2 == 0 else S.pB
                poff = 0 if m % 2 == 0 else 4
                pbufs = tuple(S.bank(poff + c)[1] for c in range(4))
                fns = []
                for j in range(G):
                    kt = s * 8 + j
                    for c in range(NCH):
                        fns.append(lambda e, m=m, c=c, j=j, kt=kt, pp=pp:
                                   e.matmul(
                            pp[:, c * 512:(c + 1) * 512],
                            wdn[:, kt, m * 128:(m + 1) * 128],
                            hid[:, j, c * 512:(c + 1) * 512],
                            start=(j == 0), stop=(j == G - 1)))
                ctx.op_multi("pe", fns,
                             reads=(wdn_b,) + tuple(hid_b), writes=pbufs)
                if e_i == 0 and s == 0:
                    ctx.op("ve", lambda e, m=m, pp=pp: e.tensor_copy(
                        yacc[:, m, :], pp[:, 0:T]),
                        reads=pbufs, writes=(yacc_b[m],))
                else:
                    ctx.op("ve", lambda e, m=m, pp=pp: e.tensor_tensor(
                        yacc[:, m, :], pp[:, 0:T], yacc[:, m, :], ALU.add),
                        reads=pbufs, writes=(yacc_b[m],))

    for g in range(G):
        ctx.op("ve", lambda e, g=g: e.tensor_tensor(
            yacc[:, g, :], yacc[:, g, :], S.xT[:, g, :], ALU.add),
            reads=(S.xT_buf[g],), writes=(yacc_b[g],))
    emit_ln(S, yacc, yacc_b, S.xT, S.xT_buf, tsc, tsc_b)


PHASES["l0f"] = emit_l0f




L1A_LIMIT = 99
L1A_YAT = True
L1A_NORM = True


def emit_l1a(S):
    ctx, nc = S.ctx, S.nc
    ph = S.stack_phase

    QT = ph.enter_context(nc.sbuf_tensor("l1_QT", [128, 8, T], BF16))
    QT_b = [Buf() for _ in range(8)]
    KT = ph.enter_context(nc.sbuf_tensor("l1_KT", [128, 8, T], BF16))
    KT_b = [Buf() for _ in range(8)]
    Vg = ph.enter_context(nc.sbuf_tensor("l1_Vg", [128, NT, 16 * 65], BF16))
    Vg_b = Buf()
    W = [ph.enter_context(nc.sbuf_tensor(f"l1_W{i}", [128, 8, 1024], BF16))
         for i in range(2)]
    W_b = [Buf(), Buf()]
    sT = [ph.enter_context(nc.sbuf_tensor(f"l1_sT{i}", [128, 1024], BF16))
          for i in range(3)]
    sT_b = [Buf(), Buf(), Buf()]
    Lb = [ph.enter_context(nc.sbuf_tensor("l1_Lb0", [128, 1024], F32))]
    Lb_b = [Buf()]
    Rb = [ph.enter_context(nc.sbuf_tensor(f"l1_Rb{i}", [128, 1024], BF16))
          for i in range(2)]
    Rb_b = [Buf(), Buf()]
    scr = ph.enter_context(nc.sbuf_tensor("l1_scr", [128, T], BF16))
    scr_b = Buf()
    nrm = ph.enter_context(nc.sbuf_tensor("l1_nrm", [65, 1024], BF16))
    nrm_b = Buf()
    rec_row = nrm[64:65, :]
    rec_b = Buf()
    bstage = ph.enter_context(nc.sbuf_tensor("l1_bst", [32, 128], BF16))
    bstage_b = Buf()
    bcm = ph.enter_context(nc.sbuf_tensor("l1_bcm", [128, 32], F32))
    bcm_b = Buf()
    bvrow = None
    bvrow_b = Buf()
    ssq = [Lb[0][:][0:16, :].bitcast(BF16)]
    ssq_b = [Lb_b[0]]
    AO, AO_b = QT, QT_b

    for i, bn in enumerate(["L1_attn_bq", "L1_attn_bk", "L1_attn_bv",
                            "L1_attn_bo"]):
        ctx.op("gp", lambda e, i=i, bn=bn: e.dma_start(
            out=bstage[8 * i:8 * i + 8, :],
            in_=S.dram[bn].rearrange("(g p) -> g p", p=128)),
            writes=(bstage_b,), q="qgp")
    bk7, bk7buf = S.bank(7)
    ctx.op("pe", lambda e, bk7=bk7: e.transpose(
        bk7[:, 0:32].bitcast(BF16)[:, 0:32], bstage[:], S.eye_bf[0:32, 0:32]),
        reads=(bstage_b, S.const_buf), writes=(bk7buf,))
    ctx.op("ve", lambda e, bk7=bk7: e.tensor_copy(
        bcm[:], bk7[:, 0:32].bitcast(BF16)[:, 0:32]),
        reads=(bk7buf,), writes=(bcm_b,))
    ctx.op("gp", lambda e: e.dma_start(
        out=nrm[0:1, :], in_=S.dram["L1_attn_bv"].unsqueeze(0)),
        writes=(bvrow_b,), q="qgp")

    def qk_proj(wname, wslot, dst, dst_b, bias_off):
        w, w_b = W[wslot], W_b[wslot]
        ctx.op("gp", lambda e, wname=wname, w=w: e.dma_start(
            out=w[:], in_=S.dram[wname].rearrange("(kt p) m -> p kt m", p=128)),
            writes=(w_b,), q="qgp")
        for mt in range(8):
            pp = S.pA if mt % 2 == 0 else S.pB
            poff = 0 if mt % 2 == 0 else 4
            pbufs = tuple(S.bank(poff + c)[1] for c in range(4))
            fns = []
            for kt in range(G):
                for c in range(NCH):
                    fns.append(lambda e, w=w, mt=mt, c=c, kt=kt, pp=pp:
                               e.matmul(
                        pp[:, c * 512:(c + 1) * 512],
                        w[:, kt, mt * 128:(mt + 1) * 128],
                        S.xT[:, kt, c * 512:(c + 1) * 512],
                        start=(kt == 0), stop=(kt == G - 1)))
            ctx.op_multi("pe", fns, reads=(w_b,) + tuple(S.xT_buf),
                         writes=pbufs)
            ctx.op("ve", lambda e, dst=dst, mt=mt, pp=pp, bias_off=bias_off:
                   e.tensor_scalar(
                dst[:, mt, :], pp[:, 0:T],
                bcm[:, bias_off + mt:bias_off + mt + 1], None,
                ALU.add, ALU.bypass),
                reads=pbufs + (bcm_b,), writes=(dst_b[mt],))

    qk_proj("L1_attn_Wq", 0, QT, QT_b, 0)
    qk_proj("L1_attn_Wk", 1, KT, KT_b, 8)
    if L1A_LIMIT < 2:
        return

    wv, wv_b = W[0], W_b[0]
    ctx.op("gp", lambda e: e.dma_start(
        out=wv[:], in_=S.dram["L1_attn_Wv"].rearrange("(kt p) m -> p kt m",
                                                      p=128)),
        writes=(wv_b,), q="qgp")
    ones_col = Vg[:].rearrange("p s (h c) -> p s h c", c=65)[:, :, :, 64:65]
    ctx.op("ve", lambda e: e.memset(ones_col, 1.0), writes=(Vg_b,))
    vgv = Vg[:].rearrange("p s (h c) -> p s h c", c=65)
    for st in range(NT):
        vbufs = (S.bank(4)[1], S.bank(5)[1])
        fns = []
        for kt in range(G):
            for j in range(2):
                fns.append(lambda e, st=st, kt=kt, j=j: e.matmul(
                    S.pB[:, j * 512:(j + 1) * 512],
                    S.xT[:, kt, st * 128:(st + 1) * 128],
                    wv[:, kt, j * 512:(j + 1) * 512],
                    start=(kt == 0), stop=False))
        for j in range(2):
            fns.append(lambda e, st=st, j=j: e.matmul(
                S.pB[:, j * 512:(j + 1) * 512], S.onesrow[0:1, 0:128],
                nrm[0:1, j * 512:(j + 1) * 512],
                start=False, stop=True))
        ctx.op_multi("pe", fns,
                     reads=(wv_b, bvrow_b, S.const_buf) + tuple(S.xT_buf),
                     writes=vbufs)
        ctx.op("ve", lambda e, st=st, vgv=vgv: e.tensor_copy(
            vgv[:, st, :, 0:64],
            S.pB[:, 0:1024].rearrange("p (h c) -> p h c", c=64)),
            reads=vbufs, writes=(Vg_b,))

    if L1A_LIMIT < 3:
        return

    def normalize(dst, dst_b, ssq_i):
        sq, sq_b = ssq[ssq_i], ssq_b[ssq_i]
        for g in range(G):
            ctx.op("ve", lambda e, g=g: e.tensor_tensor(
                scr[:], dst[:, g, :], dst[:, g, :], ALU.mult),
                reads=(dst_b[g],), writes=(scr_b,))
            ctx.op_multi("pe", [lambda e, g=g, c=c: e.matmul(
                S.pA[0:16, c * 512:(c + 1) * 512],
                S.ind[:, g * 16:(g + 1) * 16],
                scr[:, c * 512:(c + 1) * 512],
                start=(g == 0), stop=(g == G - 1)) for c in range(NCH)],
                reads=(scr_b, S.const_buf),
                writes=tuple(S.bank(c)[1] for c in range(4)))
        pa_bufs = [S.bank(c)[1] for c in range(4)]
        ctx.op("ve", lambda e, sq=sq: e.tensor_copy(sq[:], S.pA[0:16, 0:T]),
               reads=tuple(pa_bufs), writes=(sq_b,))
        ctx.op("act", lambda e, sq=sq: e.activation(sq[:], sq[:], AF.Ln),
               reads=(sq_b,), writes=(sq_b,))
        ctx.op("act", lambda e, sq=sq: e.activation(sq[:], sq[:], AF.Exp,
                                                    scale=0.5),
               reads=(sq_b,), writes=(sq_b,))
        ctx.op("ve", lambda e, sq=sq: e.tensor_scalar(
            sq[:], sq[:], 1e-8, None, ALU.add, ALU.bypass),
            reads=(sq_b,), writes=(sq_b,))
        ctx.op("act", lambda e, sq=sq: e.activation(sq[:], sq[:], AF.Ln),
               reads=(sq_b,), writes=(sq_b,))
        ctx.op("act", lambda e, sq=sq: e.activation(sq[:], sq[:], AF.Exp,
                                                    scale=-1.0),
               reads=(sq_b,), writes=(sq_b,))
        for g in range(G):
            pbufs = tuple(S.bank(4 + c)[1] for c in range(4))
            ctx.op_multi("pe", [lambda e, g=g, c=c, sq=sq: e.matmul(
                S.pB[:, c * 512:(c + 1) * 512],
                S.sel2[:, g * 128:(g + 1) * 128],
                sq[0:16, c * 512:(c + 1) * 512], start=True, stop=True)
                for c in range(NCH)],
                reads=(sq_b, S.const_buf), writes=pbufs)
            ctx.op("ve", lambda e, g=g: e.tensor_tensor(
                dst[:, g, :], dst[:, g, :], S.pB[:, 0:T], ALU.mult),
                reads=pbufs, writes=(dst_b[g],))

    normalize(QT, QT_b, 0)
    normalize(KT, KT_b, 0)

    if L1A_LIMIT < 4:
        return

    for h in range(16):
        qp, qr = h // 2, (h % 2) * 64
        for qh in range(2):
            q0 = qh * 1024
            av_bufs = [S.bank(4)[1], S.bank(5)[1]]
            def dap(st, j):
                pr = st % 3
                if pr < 2:
                    return S.pA[:, (pr * 2 + j) * 512:(pr * 2 + j + 1) * 512]
                return S.pB[:, (2 + j) * 512:(3 + j) * 512]

            def dbuf(st, j):
                return S.bank([0, 2, 6][st % 3] + j)[1]

            def dot_group(st):
                dbufs = (dbuf(st, 0), dbuf(st, 1))
                fns = [lambda e, j=j, st=st, qp=qp, qr=qr, q0=q0:
                       e.matmul(
                    dap(st, j),
                    KT[qr:qr + 64, qp, st * 128:(st + 1) * 128],
                    QT[qr:qr + 64, qp, q0 + j * 512:q0 + (j + 1) * 512],
                    start=True, stop=True) for j in range(2)]
                ctx.op_multi("pe", fns, reads=(KT_b[qp], QT_b[qp]),
                             writes=dbufs)
                return dbufs

            def av_group(st):
                sTt, sTt_b = sT[st % 3], sT_b[st % 3]
                fns = [lambda e, j=j, st=st, h=h, sTt=sTt: e.matmul(
                    S.pB[0:65, j * 512:(j + 1) * 512],
                    Vg[:, st, h * 65:(h + 1) * 65],
                    sTt[:, j * 512:(j + 1) * 512],
                    start=(st == 0), stop=(st == NT - 1)) for j in range(2)]
                ctx.op_multi("pe", fns, reads=(sTt_b, Vg_b),
                             writes=(av_bufs[0], av_bufs[1]))

            for st in range(NT):
                dbufs = dot_group(st)
                if st > 1:
                    av_group(st - 2)
                rbuf = st % 3
                sTt, sTt_b = sT[rbuf], sT_b[rbuf]
                Lt, Lt_b = Lb[0], Lb_b[0]
                Rt, Rt_b = Rb[st % 2], Rb_b[st % 2]
                ctx.op("act", lambda e, Lt=Lt, st=st: e.activation(
                    Lt[:, 0:512], dap(st, 0),
                    AF.Ln, bias=1.1, scale=-1.0),
                    reads=(dbufs[0], S.const_buf), writes=(Lt_b,))
                ctx.op("act", lambda e, Lt=Lt, st=st: e.activation(
                    Lt[:, 512:1024], dap(st, 1),
                    AF.Ln, bias=1.1, scale=-1.0),
                    reads=(dbufs[1], S.const_buf), writes=(Lt_b,))
                ctx.op("act", lambda e, Rt=Rt, Lt=Lt: e.activation(
                    Rt[:], Lt[:], AF.Exp, scale=-1.0),
                    reads=(Lt_b,), writes=(Rt_b,))
                ctx.op("ve", lambda e, sTt=sTt, Rt=Rt, st=st:
                       e.scalar_tensor_tensor(
                    sTt[:, 0:512], Rt[:, 0:512], 4.41, dap(st, 0),
                    ALU.mult, ALU.subtract),
                    reads=(Rt_b, dbufs[0]), writes=(sTt_b,))
                ctx.op("ve", lambda e, sTt=sTt, Rt=Rt, st=st:
                       e.scalar_tensor_tensor(
                    sTt[:, 512:1024], Rt[:, 512:1024], 4.41, dap(st, 1),
                    ALU.mult, ALU.subtract),
                    reads=(Rt_b, dbufs[1]), writes=(sTt_b,))
                ctx.op("ve", lambda e, sTt=sTt: e.tensor_scalar(
                    sTt[:], sTt[:], -3.1, None, ALU.add, ALU.bypass),
                    reads=(), writes=(sTt_b,))
            av_group(NT - 2)
            av_group(NT - 1)
            if L1A_NORM:
                ctx.op("act", lambda e: e.activation(
                    rec_row[0:1, :], S.pB[64:65, 0:1024], AF.Ln, bias=1e-6),
                    reads=tuple(av_bufs) + (S.const_buf,), writes=(rec_b,))
                ctx.op("act", lambda e: e.activation(
                    rec_row[0:1, :], rec_row[0:1, :], AF.Exp, scale=-1.0),
                    reads=(rec_b,), writes=(rec_b,))
            else:
                ctx.op("ve", lambda e: e.memset(rec_row[0:1, :], 0.001),
                       reads=tuple(av_bufs), writes=(rec_b,))
            for j in range(2):
                bk, bkbuf = S.bank(6 + j)
                ctx.op("pe", lambda e, j=j, bk=bk: e.matmul(
                    bk[0:64, :], S.onesrow[64:65, 0:64],
                    rec_row[0:1, j * 512:(j + 1) * 512], start=True, stop=True),
                    reads=(rec_b, S.const_buf), writes=(bkbuf,))
            b6, b6buf = S.bank(6)
            b7, b7buf = S.bank(7)
            ctx.op("ve", lambda e, b6=b6: e.tensor_copy(nrm[0:64, 0:512],
                                                        b6[0:64, :]),
                   reads=(b6buf,), writes=(nrm_b, bvrow_b))
            ctx.op("ve", lambda e, b7=b7: e.tensor_copy(nrm[0:64, 512:1024],
                                                        b7[0:64, :]),
                   reads=(b7buf,), writes=(nrm_b,))
            ctx.op("ve", lambda e, qr=qr, qp=qp, q0=q0: e.tensor_tensor(
                AO[qr:qr + 64, qp, q0:q0 + 1024], S.pB[0:64, 0:1024],
                nrm[0:64, :], ALU.mult),
                reads=(nrm_b,) + tuple(av_bufs), writes=(AO_b[qp],))

    if L1A_LIMIT < 5:
        return

    wo, wo_b = W[1], W_b[1]
    ctx.op("gp", lambda e: e.dma_start(
        out=wo[:], in_=S.dram["L1_attn_Wo"].rearrange("(kt p) m -> p kt m",
                                                      p=128)),
        writes=(wo_b,), q="qgp")
    r, r_b = KT, KT_b
    for mt in range(8):
        for c in range(NCH):
            bk, bkbuf = S.bank((mt * NCH + c) % 4)
            for kt in range(G):
                ctx.op("pe", lambda e, mt=mt, c=c, kt=kt, bk=bk: e.matmul(
                    bk, wo[:, kt, mt * 128:(mt + 1) * 128],
                    AO[:, kt, c * 512:(c + 1) * 512],
                    start=(kt == 0), stop=(kt == G - 1)),
                    reads=(wo_b, AO_b[kt]), writes=(bkbuf,))
            ctx.op("ve", lambda e, mt=mt, c=c, bk=bk: e.tensor_scalar(
                scr[:, c * 512:(c + 1) * 512], bk,
                bcm[:, 24 + mt:24 + mt + 1], None, ALU.add, ALU.bypass),
                reads=(bkbuf, bcm_b), writes=(scr_b,))
            ctx.op("ve", lambda e, mt=mt, c=c: e.tensor_tensor(
                r[:, mt, c * 512:(c + 1) * 512],
                scr[:, c * 512:(c + 1) * 512],
                S.xT[:, mt, c * 512:(c + 1) * 512], ALU.add),
                reads=(scr_b, S.xT_buf[mt]), writes=(r_b[mt],))

    emit_ln(S, r, r_b, S.xT, S.xT_buf, scr, scr_b)


PHASES["l1a"] = emit_l1a




def emit_l1f(S):
    ctx, nc = S.ctx, S.nc
    ph = S.stack_phase
    g8 = ph.enter_context(nc.sbuf_tensor("f1_g8", [8, T], BF16))
    g8_b = Buf()
    S.sele = ph.enter_context(nc.sbuf_tensor("sele_l1", [8, 8 * 128], BF16))
    ctx.op("gp", lambda e: e.dma_start(out=S.sele[:], in_=S.dram["C_sele"]),
           writes=(S.const_buf,), q="qgp")
    emit_router(S, "L1_ffn_Wg", "L1_ffn_bg", 5.0, g8, g8_b, 0.2)

    yacc = ph.enter_context(nc.sbuf_tensor("f1_yacc", [128, G, T], BF16))
    yacc_b = [Buf() for _ in range(G)]
    hid = ph.enter_context(nc.sbuf_tensor("f1_hid", [128, 8, T], BF16))
    hid_b = [Buf() for _ in range(8)]
    NS1 = 4
    w1s = [ph.enter_context(nc.sbuf_tensor(f"f1_w1{i}", [128, 8, 128], BF16))
           for i in range(NS1)]
    w1s_b = [Buf() for _ in range(NS1)]
    w2e = ph.enter_context(nc.sbuf_tensor("f1_w2e", [128, 16, 1024], BF16))
    w2e_b = Buf()
    gbc = [ph.enter_context(nc.sbuf_tensor(f"f1_gbc{i}", [128, T], BF16))
           for i in range(2)]
    gbc_b = [Buf(), Buf()]
    tsc = ph.enter_context(nc.sbuf_tensor("f1_tsc", [128, T], BF16))
    tsc_b = Buf()
    b1st = ph.enter_context(nc.sbuf_tensor("f1_b1st", [16, 128], F32))
    b1st_b = Buf()
    b1cm = ph.enter_context(nc.sbuf_tensor("f1_b1cm", [128, 8, 16], F32))
    b1cm_b = Buf()
    b2sb = ph.enter_context(nc.sbuf_tensor("f1_b2", [8, 1024], BF16))
    b2sb_b = Buf()
    ctx.op("gp", lambda e: e.dma_start(out=b2sb[:], in_=S.dram["L1_ffn_b2"]),
           writes=(b2sb_b,), q="qgp")
    for e_i in range(8):
        ctx.op("gp", lambda e, e_i=e_i: e.dma_start(
            out=b1st[:], in_=S.dram["L1_ffn_b1"][e_i].rearrange(
                "(a p) -> a p", p=128)),
            writes=(b1st_b,), q="qgp")
        bk7, bk7buf = S.bank(7)
        ctx.op("pe", lambda e, bk7=bk7: e.transpose(
            bk7[:, 0:16], b1st[:], S.eye_f32[0:16, 0:16]),
            reads=(b1st_b, S.const_buf), writes=(bk7buf,))
        ctx.op("ve", lambda e, e_i=e_i, bk7=bk7: e.tensor_copy(
            b1cm[:, e_i, :], bk7[:, 0:16]),
            reads=(bk7buf,), writes=(b1cm_b,))

    W1 = S.dram["L1_ffn_W1"]
    W2 = S.dram["L1_ffn_W2"]
    HD = 2048

    def dma_w1(e_i, i_g):
        slot = i_g % NS1
        w, w_b = w1s[slot], w1s_b[slot]
        ctx.op("gp", lambda e, e_i=e_i, i_g=i_g, w=w: e.dma_start(
            out=w[:], in_=W1[e_i, :, i_g * 128:(i_g + 1) * 128]
            .rearrange("(kt p) m -> p kt m", p=128)),
            writes=(w_b,), q="qgp")

    for e_i in range(8):
        dma_w1(e_i, 0)
        dma_w1(e_i, 1)
        ctx.op("gp", lambda e, e_i=e_i: e.dma_start(
            out=w2e[:, 0:8, :],
            in_=W2[e_i, 0:1024, :].rearrange("(kt p) m -> p kt m", p=128)),
            writes=(w2e_b,), q="qgp")
        dma_w1(e_i, 2)
        dma_w1(e_i, 3)
        ctx.op("gp", lambda e, e_i=e_i: e.dma_start(
            out=w2e[:, 8:16, :],
            in_=W2[e_i, 1024:2048, :].rearrange("(kt p) m -> p kt m", p=128)),
            writes=(w2e_b,), q="qgp")
        gb, gb_buf = gbc[e_i % 2], gbc_b[e_i % 2]
        ctx.op_multi("pe", [lambda e, e_i=e_i, c=c: e.matmul(
            S.pB[:, c * 512:(c + 1) * 512],
            S.sele[:, e_i * 128:(e_i + 1) * 128],
            g8[0:8, c * 512:(c + 1) * 512], start=True, stop=True)
            for c in range(NCH)],
            reads=(g8_b, S.const_buf),
            writes=tuple(S.bank(4 + c)[1] for c in range(4)))
        ctx.op("ve", lambda e, gb=gb: e.tensor_copy(gb[:], S.pB[:, 0:T]),
               reads=tuple(S.bank(4 + c)[1] for c in range(4)),
               writes=(gb_buf,))
        for s in range(2):
            for j in range(8):
                i_g = s * 8 + j
                slot = i_g % NS1
                w, w_b = w1s[slot], w1s_b[slot]
                if i_g >= 4:
                    dma_w1(e_i, i_g)
                pp = S.pA if j % 2 == 0 else S.pB
                poff = 0 if j % 2 == 0 else 4
                pbufs = tuple(S.bank(poff + c)[1] for c in range(4))
                fns = []
                for kt in range(G):
                    for c in range(NCH):
                        fns.append(lambda e, c=c, kt=kt, w=w, pp=pp: e.matmul(
                            pp[:, c * 512:(c + 1) * 512], w[:, kt, :],
                            S.xT[:, kt, c * 512:(c + 1) * 512],
                            start=(kt == 0), stop=(kt == G - 1)))
                ctx.op_multi("pe", fns, reads=(w_b,) + tuple(S.xT_buf),
                             writes=pbufs)
                ctx.op("act", lambda e, pp=pp, e_i=e_i, i_g=i_g: e.activation(
                    tsc[:], pp[:, 0:T], AF.Gelu,
                    bias=b1cm[:, e_i, i_g:i_g + 1]),
                    reads=pbufs + (b1cm_b,), writes=(tsc_b,))
                ctx.op("ve", lambda e, j=j, gb=gb: e.tensor_tensor(
                    hid[:, j, :], tsc[:], gb[:], ALU.mult),
                    reads=(tsc_b, gb_buf), writes=(hid_b[j],))
            first = (e_i == 0 and s == 0)
            for m in range(G):
                pp = S.pA if m % 2 == 0 else S.pB
                poff = 0 if m % 2 == 0 else 4
                pbufs = tuple(S.bank(poff + c)[1] for c in range(4))
                fns = []
                for j in range(G):
                    for c in range(NCH):
                        stop = (j == G - 1) and not first
                        fns.append(lambda e, m=m, c=c, j=j, s=s, pp=pp,
                                   stop=stop: e.matmul(
                            pp[:, c * 512:(c + 1) * 512],
                            w2e[:, s * 8 + j, m * 128:(m + 1) * 128],
                            hid[:, j, c * 512:(c + 1) * 512],
                            start=(j == 0), stop=stop))
                if first:
                    for c in range(NCH):
                        fns.append(lambda e, m=m, c=c, pp=pp: e.matmul(
                            pp[:, c * 512:(c + 1) * 512],
                            b2sb[0:8, m * 128:(m + 1) * 128],
                            g8[0:8, c * 512:(c + 1) * 512],
                            start=False, stop=True))
                ctx.op_multi("pe", fns,
                             reads=(w2e_b, b2sb_b, g8_b) + tuple(hid_b),
                             writes=pbufs)
                if first:
                    ctx.op("ve", lambda e, m=m, pp=pp: e.tensor_copy(
                        yacc[:, m, :], pp[:, 0:T]),
                        reads=pbufs, writes=(yacc_b[m],))
                else:
                    ctx.op("ve", lambda e, m=m, pp=pp: e.tensor_tensor(
                        yacc[:, m, :], pp[:, 0:T], yacc[:, m, :], ALU.add),
                        reads=pbufs, writes=(yacc_b[m],))

    for g in range(G):
        ctx.op("ve", lambda e, g=g: e.tensor_tensor(
            yacc[:, g, :], yacc[:, g, :], S.xT[:, g, :], ALU.add),
            reads=(S.xT_buf[g],), writes=(yacc_b[g],))
    emit_ln(S, yacc, yacc_b, S.xT, S.xT_buf, tsc, tsc_b)


PHASES["l1f"] = emit_l1f


def emit_identity_f32(S):
    ctx, nc = S.ctx, S.nc
    x4 = S.stack_phase.enter_context(nc.sbuf_tensor("x4", [128, G, T], F32))
    x4_buf = [Buf() for _ in range(G)]
    for g in range(G):
        ctx.op("ve", lambda e, g=g: e.tensor_copy(x4[:, g, :], S.xT[:, g, :]),
               reads=(S.xT_buf[g],), writes=(x4_buf[g],))
    return x4, x4_buf


def build_program(segs, weight_names):
    nc = bass.Bass("TRN2", target_bir_lowering=False)
    S = State()
    S.nc = nc
    S.dram = {}

    first, last = segs[0], segs[-1]
    if first == "in":
        S.dram["x"] = nc.dram_tensor("x", [T, D], F32, kind="ExternalInput").ap()
    else:
        S.dram["xT_in"] = nc.dram_tensor("xT_in", [128, G * T], BF16,
                                         kind="ExternalInput").ap()
    for w, shape in weight_names:
        S.dram[w] = nc.dram_tensor(w, list(shape), F32, kind="ExternalInput").ap()
    cdefs = host_constants()
    for name in cdefs:
        arr = cdefs[name]
        dt = F32 if arr.dtype == np.float32 else BF16
        S.dram[name] = nc.dram_tensor(name, list(arr.shape), dt,
                                      kind="ExternalInput").ap()
    if last == "out":
        S.dram["out"] = nc.dram_tensor("out", [T, D], F32, kind="ExternalOutput").ap()
    else:
        S.dram["xT_out"] = nc.dram_tensor("xT_out", [128, G * T], BF16,
                                          kind="ExternalOutput").ap()

    with ExitStack() as stack:
        ctx = Ctx(nc, stack)
        S.ctx = ctx
        alloc_psum(nc, stack, S)
        alloc_consts(nc, stack, S)
        S.xT = stack.enter_context(nc.sbuf_tensor("xT", [128, G, T], BF16))
        S.xT_buf = [Buf() for _ in range(G)]

        def run_phase(fn, name=None):
            ctx.phase = name or getattr(fn, "__name__", "phase")
            with ExitStack() as stack_phase:
                S.stack_phase = stack_phase
                fn(S)
            ctx.phase_barrier()

        if first == "in":
            run_phase(emit_load_xT_from_x)
        else:
            run_phase(emit_load_xT_handoff)

        for seg in segs:
            if seg in ("in", "out", "handoff"):
                continue
            run_phase(PHASES[seg], seg)

        if last == "out":
            ctx.phase = "outp"
            with ExitStack() as stack_phase:
                S.stack_phase = stack_phase
                x4, x4_buf = (S.final_f32 if hasattr(S, "final_f32")
                              else emit_identity_f32(S))
                emit_store_out_from_f32(S, x4, x4_buf)
        else:
            ctx.phase = "outp"
            emit_store_handoff(S)

        with nc.Block() as block:
            ctx.finalize(block)

    return nc, list(S.dram.keys())


WEIGHTS = {
    "l0a": [("L0_attn_Wq", (D, 1024)), ("L0_attn_Wk", (D, 256)),
            ("L0_attn_Wv", (D, 256)), ("L0_attn_Wo", (1024, D))],
    "l0f": [("L0_ffn_gate_w", (D, 8)), ("L0_ffn_gate_up_w", (8, D, 4096)),
            ("L0_ffn_down_w", (8, 2048, D))],
    "l1a": [("L1_attn_Wq", (D, D)), ("L1_attn_bq", (D,)),
            ("L1_attn_Wk", (D, D)), ("L1_attn_bk", (D,)),
            ("L1_attn_Wv", (D, D)), ("L1_attn_bv", (D,)),
            ("L1_attn_Wo", (D, D)), ("L1_attn_bo", (D,))],
    "l1f": [("L1_ffn_Wg", (D, 8)), ("L1_ffn_bg", (8,)),
            ("L1_ffn_W1", (8, D, 2048)), ("L1_ffn_b1", (8, 2048)),
            ("L1_ffn_W2", (8, 2048, D)), ("L1_ffn_b2", (8, D))],
}


def weights_for(segs):
    out = []
    for s in segs:
        out.extend(WEIGHTS.get(s, []))
    return out


LAUNCH_PLAN = [["in", "l0a", "l0f", "l1a", "l1f", "out"]]


def kernel(**inputs):
    consts = host_constants()
    x_full = np.ascontiguousarray(inputs["L0_attn_x"])
    n_cores = 8

    carry = [None] * n_cores
    out = None
    for li, segs in enumerate(LAUNCH_PLAN):
        wnames = weights_for(segs)
        nc, names = build_program(segs, wnames)
        in_maps = []
        for c in range(n_cores):
            m = {}
            if segs[0] == "in":
                m["x"] = np.ascontiguousarray(x_full[c])
            else:
                m["xT_in"] = carry[c]
            for w, _ in wnames:
                m[w] = np.ascontiguousarray(inputs[w])
            for k, v in consts.items():
                m[k] = v
            in_maps.append(m)
        res = run_bass_kernel_spmd(nc, in_maps, core_ids=list(range(n_cores)))
        if segs[-1] == "out":
            out = np.stack([res.results[c]["out"] for c in range(n_cores)], axis=0)
        else:
            carry = [np.ascontiguousarray(res.results[c]["xT_out"])
                     for c in range(n_cores)]
    return out


# version 30
# speedup vs baseline: 1.0536x; 1.0536x over previous
import math
from contextlib import ExitStack

import numpy as np
import ml_dtypes

import concourse.bass as bass
import concourse.mybir as mybir
from concourse.bass_utils import run_bass_kernel_spmd

F32 = mybir.dt.float32
BF16 = mybir.dt.bfloat16
AF = mybir.ActivationFunctionType
ALU = mybir.AluOpType
AX = mybir.AxisListType

T = 2048
D = 1024
G = D // 128
NT = T // 128
NC_ = 512
NCH = T // NC_

ENGINES = ("pe", "act", "ve", "gp", "sp")
QFAN = {"qsp": 2, "qgp": 4, "qact": 1}
DMAQ = tuple(f"{q}{i}" for q, n in QFAN.items() for i in range(n))

PHASES = {}


class Buf:
    __slots__ = ("w", "r")

    def __init__(self):
        self.w = None
        self.r = []


class Ctx:
    def __init__(self, nc, stack):
        self.nc = nc
        self.sems = {}
        for s in ENGINES + DMAQ:
            self.sems[s] = stack.enter_context(nc.semaphore("sem_" + s))
        self.cnt = {s: 0 for s in ENGINES + DMAQ}
        self.clock = {e: {s: 0 for s in ENGINES + DMAQ} for e in ENGINES}
        self.ops = {e: [] for e in ENGINES}
        self.phase = "init"
        self.qrr = {q: 0 for q in QFAN}

    def _deps(self, reads, writes, extra):
        deps = {}

        def add(tok):
            if tok is None:
                return
            s, c = tok
            if deps.get(s, 0) < c:
                deps[s] = c

        for b in reads:
            add(b.w)
        for b in writes:
            add(b.w)
            for t in b.r:
                add(t)
        for t in extra:
            add(t)
        return deps

    def op(self, eng, fn, reads=(), writes=(), extra=(), q=None):
        if q is not None:
            i = self.qrr[q]
            self.qrr[q] = (i + 1) % QFAN[q]
            q = f"{q}{i}"
            extra = tuple(extra) + ((q, self.cnt[q]),)
        deps = self._deps(reads, writes, extra)
        waits = {}
        clk = self.clock[eng]
        for s, c in deps.items():
            if clk[s] < c:
                waits[s] = c
                clk[s] = c
        stream = q if q is not None else eng
        inc = 16 if q is not None else 1
        self.cnt[stream] += inc
        tok = (stream, self.cnt[stream])
        self.ops[eng].append((waits, fn, stream, inc, self.phase))
        for b in reads:
            b.r.append(tok)
        for b in writes:
            b.w = tok
            b.r = []
        return tok

    def op_multi(self, eng, fns, reads=(), writes=(), extra=(), q=None):
        return self.op(eng, list(fns), reads=reads, writes=writes, extra=extra,
                       q=q)

    def phase_barrier(self):
        snap = dict(self.cnt)
        for e in ENGINES:
            clk = self.clock[e]
            waits = {}
            for s, c in snap.items():
                if clk[s] < c:
                    waits[s] = c
                    clk[s] = c
            if waits:
                self.ops[e].append((waits, None, None, 0, self.phase))

    def barrier(self, eng, tokens):
        deps = self._deps((), (), tokens)
        clk = self.clock[eng]
        waits = {}
        for s, c in deps.items():
            if clk[s] < c:
                waits[s] = c
                clk[s] = c
        if waits:
            self.ops[eng].append((waits, None, None, 0, self.phase))

    def finalize(self, block):
        engmap = {"pe": block.tensor, "act": block.scalar, "ve": block.vector,
                  "gp": block.gpsimd, "sp": block.sync}
        sems = self.sems
        for e in ENGINES:
            oplist = self.ops[e]
            if not oplist:
                continue

            def body(engine, oplist=oplist):
                cur = None
                sid = None
                for waits, fn, stream, inc, phase in oplist:
                    if phase != cur:
                        if cur is not None:
                            self.nc.leave_named_scope(cur, sid, False)
                        sid, _ = self.nc.enter_named_scope(phase, False)
                        cur = phase
                    for s, c in waits.items():
                        engine.wait_ge(sems[s], c)
                    if fn is None:
                        continue
                    if isinstance(fn, list):
                        for f in fn[:-1]:
                            f(engine)
                        inst = fn[-1](engine)
                    else:
                        inst = fn(engine)
                    inst.then_inc(sems[stream], inc)
                if cur is not None:
                    self.nc.leave_named_scope(cur, sid, False)

            engmap[e](body)


def host_constants():
    c = {}
    c["C_eye_bf"] = np.eye(128, dtype=ml_dtypes.bfloat16)
    c["C_eye_f32"] = np.eye(128, dtype=np.float32)
    ones2 = np.zeros((128, 2), dtype=ml_dtypes.bfloat16)
    ones2[0:64, 0] = 1
    ones2[64:128, 1] = 1
    c["C_ones2"] = ones2
    c["C_ones1"] = np.ones((128, 1), dtype=ml_dtypes.bfloat16)
    c["C_onesrow"] = np.ones((65, 128), dtype=ml_dtypes.bfloat16)
    ind = np.zeros((128, G, 16), dtype=ml_dtypes.bfloat16)
    for g in range(G):
        ind[0:64, g, 2 * g] = 1
        ind[64:128, g, 2 * g + 1] = 1
    c["C_ind"] = ind.reshape(128, G * 16)
    sel2 = np.zeros((16, G, 128), dtype=ml_dtypes.bfloat16)
    for g in range(G):
        sel2[2 * g, g, 0:64] = 1
        sel2[2 * g + 1, g, 64:128] = 1
    c["C_sel2"] = sel2.reshape(16, G * 128)
    sele = np.zeros((8, 8, 128), dtype=ml_dtypes.bfloat16)
    for e in range(8):
        sele[e, e, :] = 1
    c["C_sele"] = sele.reshape(8, 8 * 128)
    dh = 64
    inv = 1.0 / (10000.0 ** (np.arange(0, dh, 2, dtype=np.float64) / dh))
    pos = np.arange(T, dtype=np.float64)
    ang = pos[None, :] * inv[:, None]
    cos32 = np.cos(ang)
    sin32 = np.sin(ang)
    cosf = np.concatenate([cos32, cos32], axis=0)
    sinf = np.concatenate([sin32, sin32], axis=0)
    c["C_cos"] = np.concatenate([cosf, cosf], axis=0).astype(np.float32)
    c["C_sin"] = np.concatenate([sinf, sinf], axis=0).astype(np.float32)
    rot = np.zeros((128, 128), dtype=ml_dtypes.bfloat16)
    for base in (0, 64):
        for j in range(32):
            rot[base + j + 32, base + j] = -1.0
            rot[base + j, base + j + 32] = 1.0
    c["C_rot"] = rot
    c["C_cvals"] = np.tile(np.array([[1e-5, 1e-6, 1.1, 1.0, 0.1, 0.0, 0.5, 2.0]],
                                    dtype=np.float32), (128, 1))
    return c


class State:
    pass


def alloc_psum(nc, stack, S):
    S.pA = stack.enter_context(nc.psum_tensor("pA", [128, 2048], F32))
    S.pB = stack.enter_context(nc.psum_tensor("pB", [128, 2048], F32))
    S.pA_buf = [Buf() for _ in range(4)]
    S.pB_buf = [Buf() for _ in range(4)]

    def bank(i):
        if i < 4:
            return S.pA[:, 512 * i:512 * (i + 1)], S.pA_buf[i]
        return S.pB[:, 512 * (i - 4):512 * (i - 3)], S.pB_buf[i - 4]

    S.bank = bank


def alloc_consts(nc, stack, S):
    ap = lambda h: h[:]
    S.eye_bf = stack.enter_context(nc.sbuf_tensor("eye_bf", [128, 128], BF16))
    S.eye_f32 = stack.enter_context(nc.sbuf_tensor("eye_f32", [128, 128], F32))
    S.ones2 = stack.enter_context(nc.sbuf_tensor("ones2", [128, 2], BF16))
    S.ones1 = stack.enter_context(nc.sbuf_tensor("ones1", [128, 1], BF16))
    S.onesrow = stack.enter_context(nc.sbuf_tensor("onesrow", [65, 128], BF16))
    S.ind = stack.enter_context(nc.sbuf_tensor("ind", [128, G * 16], BF16))
    S.sel2 = stack.enter_context(nc.sbuf_tensor("sel2", [16, G * 128], BF16))
    S.rows = stack.enter_context(nc.sbuf_tensor("rows", [1, T], F32))
    S.rows_buf = Buf()
    S.rowsb = stack.enter_context(nc.sbuf_tensor("rowsb", [1, 3 * T], BF16))
    S.rowsb_buf = Buf()
    S.rot = stack.enter_context(nc.sbuf_tensor("rot", [128, 128], BF16))
    S.cvals = stack.enter_context(nc.sbuf_tensor("cvals", [128, 8], F32))
    CVALS = [1e-5, 1e-6, 1.1, 1.0, 0.1, 0.0, 0.5, 2.0]
    for i, v in enumerate(CVALS):
        nc.const_aps.register(F32, v, S.cvals[:, i:i + 1])
    S.const_buf = Buf()
    names = ["C_eye_bf", "C_eye_f32", "C_ones2", "C_ones1", "C_onesrow",
             "C_ind", "C_sel2", "C_rot", "C_cvals"]
    handles = [S.eye_bf, S.eye_f32, S.ones2, S.ones1, S.onesrow, S.ind,
               S.sel2, S.rot, S.cvals]
    for name, h in zip(names, handles):
        d = S.dram[name]
        S.ctx.op("gp", lambda e, h=h, d=d: e.dma_start(out=h[:], in_=d),
                 writes=(S.const_buf,), q="qgp")


def emit_load_xT_from_x(S):
    ctx, nc = S.ctx, S.nc
    xrows = S.stack_phase.enter_context(nc.sbuf_tensor("p0_xrows", [128, NT * D], BF16))
    xrows_buf = Buf()
    src = S.dram["x"].rearrange("(t p) d -> p t d", p=128)
    dst = xrows[:].rearrange("p (t d) -> p t d", d=D)
    ctx.op("gp", lambda e: e.dma_start(out=dst, in_=src),
           writes=(xrows_buf,), q="qgp")
    for t in range(NT):
        for g in range(G):
            bk, bkbuf = S.bank((t * G + g) % 8)
            bkb = bk.bitcast(BF16)
            ctx.op("pe", lambda e, t=t, g=g, bkb=bkb: e.transpose(
                bkb[:, 0:128], xrows[:, t * D + g * 128: t * D + (g + 1) * 128],
                S.eye_bf[:]),
                reads=(xrows_buf, S.const_buf), writes=(bkbuf,))
            ctx.op("ve", lambda e, t=t, g=g, bkb=bkb: e.tensor_copy(
                S.xT[:, g, t * 128:(t + 1) * 128], bkb[:, 0:128]),
                reads=(bkbuf,), writes=(S.xT_buf[g],))


def emit_load_xT_handoff(S):
    ctx = S.ctx
    src = S.dram["xT_in"]
    for g in range(G):
        ctx.op("gp", lambda e, g=g: e.dma_start(
            out=S.xT[:, g, :], in_=src[:, g * T:(g + 1) * T]),
            writes=(S.xT_buf[g],), q="qgp")


def emit_store_handoff(S):
    ctx = S.ctx
    dst = S.dram["xT_out"]
    toks = []
    for g in range(G):
        toks.append(ctx.op("sp", lambda e, g=g: e.dma_start(
            out=dst[:, g * T:(g + 1) * T], in_=S.xT[:, g, :]),
            reads=(S.xT_buf[g],), q="qsp"))
    ctx.barrier("sp", toks)


def emit_store_out_from_f32(S, x4, x4_buf):
    ctx, nc = S.ctx, S.nc
    rowbuf = [S.stack_phase.enter_context(nc.sbuf_tensor(f"p5_row{i}", [128, D], F32))
              for i in range(2)]
    rowbuf_buf = [Buf(), Buf()]
    out = S.dram["out"]
    toks = []
    for t in range(NT):
        rb = rowbuf[t % 2]
        rbb = rowbuf_buf[t % 2]
        for g in range(G):
            bk, bkbuf = S.bank((t * G + g) % 8)
            ctx.op("pe", lambda e, t=t, g=g, bk=bk: e.transpose(
                bk[:, 0:128], x4[:, g, t * 128:(t + 1) * 128], S.eye_f32[:]),
                reads=(x4_buf[g], S.const_buf), writes=(bkbuf,))
            ctx.op("ve", lambda e, g=g, bk=bk, rb=rb: e.tensor_copy(
                rb[:, g * 128:(g + 1) * 128], bk[:, 0:128]),
                reads=(bkbuf,), writes=(rbb,))
        eng, qq = ("sp", "qsp") if t % 2 == 0 else ("gp", "qgp")
        toks.append(ctx.op(eng, lambda e, t=t, rb=rb: e.dma_start(
            out=out[t * 128:(t + 1) * 128, :], in_=rb[:]),
            reads=(rbb,), q=qq))
    ctx.barrier("sp", toks)


def load_weight_2d(S, name, nkt, nmt, mt_w=128, tag=None):
    nc = S.nc
    tag = tag or name
    h = S.stack_phase.enter_context(
        nc.sbuf_tensor("w_" + tag, [128, nkt, nmt * mt_w], BF16))
    buf = Buf()
    src = S.dram[name].rearrange("(kt p) m -> p kt m", p=128)
    S.ctx.op("gp", lambda e: e.dma_start(out=h[:], in_=src), writes=(buf,), q="qgp")
    return h, buf


def emit_ln(S, r, r_buf, dst, dst_buf, scratch, scratch_buf, out_f32=False):
    ctx, nc = S.ctx, S.nc
    allbanks = [S.bank(i)[1] for i in range(8)]
    for g in range(G):
        ctx.op("ve", lambda e, g=g: e.tensor_tensor(
            scratch[:], r[:, g, :], r[:, g, :], ALU.mult),
            reads=(r_buf[g],), writes=(scratch_buf,))
        fns = []
        for c in range(NCH):
            fns.append(lambda e, g=g, c=c: e.matmul(
                S.pA[0:1, c * 512:(c + 1) * 512], S.ones1[:],
                r[:, g, c * 512:(c + 1) * 512],
                start=(g == 0), stop=(g == G - 1)))
        for c in range(NCH):
            fns.append(lambda e, g=g, c=c: e.matmul(
                S.pB[0:1, c * 512:(c + 1) * 512], S.ones1[:],
                scratch[:, c * 512:(c + 1) * 512],
                start=(g == 0), stop=(g == G - 1)))
        ctx.op_multi("pe", fns, reads=(r_buf[g], scratch_buf, S.const_buf),
                     writes=tuple(allbanks))
    rows_buf = S.rows_buf
    rstdb = S.rowsb_buf

    def rows0(sl=None):
        a = sl.start if sl else 0
        b = sl.stop if sl else T
        return S.rows[0:1, a:b]

    def rowsb(i, sl=None):
        a = i * T + (sl.start if sl else 0)
        b = i * T + (sl.stop if sl else T)
        return S.rowsb[0:1, a:b]

    pa_bufs = [S.bank(c)[1] for c in range(4)]
    pb_bufs = [S.bank(4 + c)[1] for c in range(4)]
    ctx.op("ve", lambda e: e.tensor_scalar(rowsb(0), S.pA[0:1, 0:T], 1.0 / D,
                                           None, ALU.mult, ALU.bypass),
           reads=tuple(pa_bufs), writes=(rstdb,))
    ctx.op("ve", lambda e: e.tensor_copy(rows0(), S.pB[0:1, 0:T]),
           reads=tuple(pb_bufs), writes=(rows_buf,))
    ctx.op("act", lambda e: e.activation(S.pA[0:1, 0:T], rowsb(0), AF.Square),
           reads=(rstdb,), writes=tuple(pa_bufs))
    ctx.op("ve", lambda e: e.scalar_tensor_tensor(
        rows0(), rows0(), 1.0 / D, S.pA[0:1, 0:T], ALU.mult, ALU.subtract),
        reads=(rows_buf,) + tuple(pa_bufs), writes=(rows_buf,))
    ctx.op("act", lambda e: e.activation(rows0(), rows0(), AF.Ln, bias=1e-5),
           reads=(rows_buf, S.const_buf), writes=(rows_buf,))
    ctx.op("act", lambda e: e.activation(rowsb(1), rows0(), AF.Exp, scale=-0.5),
           reads=(rows_buf,), writes=(rstdb,))
    ctx.op("ve", lambda e: e.scalar_tensor_tensor(
        rowsb(2), rowsb(0), -1.0, rowsb(1), ALU.mult, ALU.mult),
        reads=(rstdb,), writes=(rstdb,))
    fns = []
    for c in range(NCH):
        fns.append(lambda e, c=c: e.matmul(
            S.pA[:, c * 512:(c + 1) * 512], S.onesrow[0:1, :],
            rowsb(1, slice(c * 512, (c + 1) * 512)), start=True, stop=True))
    for c in range(NCH):
        fns.append(lambda e, c=c: e.matmul(
            S.pB[:, c * 512:(c + 1) * 512], S.onesrow[0:1, :],
            rowsb(2, slice(c * 512, (c + 1) * 512)), start=True, stop=True))
    ctx.op_multi("pe", fns, reads=(rstdb, S.const_buf), writes=tuple(allbanks))
    pa_bufs = [S.bank(c)[1] for c in range(4)]
    pb_bufs = [S.bank(4 + c)[1] for c in range(4)]
    for g in range(G):
        ctx.op("ve", lambda e, g=g: e.tensor_tensor(
            scratch[:], r[:, g, :], S.pA[:, 0:T], ALU.mult),
            reads=(r_buf[g],) + tuple(pa_bufs), writes=(scratch_buf,))
        ctx.op("ve", lambda e, g=g: e.tensor_tensor(
            dst[:, g, :], scratch[:], S.pB[:, 0:T], ALU.add),
            reads=(scratch_buf,) + tuple(pb_bufs), writes=(dst_buf[g],))


def emit_l0a(S):
    ctx, nc = S.ctx, S.nc
    ph = S.stack_phase
    wq, wq_b = load_weight_2d(S, "L0_attn_Wq", 8, 8)
    wk, wk_b = load_weight_2d(S, "L0_attn_Wk", 8, 2)
    wv, wv_b = load_weight_2d(S, "L0_attn_Wv", 8, 2)
    wo, wo_b = wq, wq_b

    QT = ph.enter_context(nc.sbuf_tensor("l0_QT", [128, 8, T], BF16))
    QT_b = [Buf() for _ in range(8)]
    KTD = ph.enter_context(nc.sbuf_tensor("l0_KTD", [128, 4, T], BF16))
    KTD_b = [Buf() for _ in range(4)]
    Vg = ph.enter_context(nc.sbuf_tensor("l0_Vg", [128, NT, 4 * 65], BF16))
    Vg_b = Buf()
    AO = ph.enter_context(nc.sbuf_tensor("l0_AO", [128, 8, T], BF16))
    AO_b = [Buf() for _ in range(8)]
    cos_ap = AO[:, 0:2, :].bitcast(F32).rearrange("p a b -> p (a b)")
    sin_ap = AO[:, 2:4, :].bitcast(F32).rearrange("p a b -> p (a b)")
    tab_b = Buf()
    ctx.op("gp", lambda e: e.dma_start(out=cos_ap, in_=S.dram["C_cos"]),
           writes=(tab_b,), q="qgp")
    ctx.op("gp", lambda e: e.dma_start(out=sin_ap, in_=S.dram["C_sin"]),
           writes=(tab_b,), q="qgp")
    expS = [ph.enter_context(nc.sbuf_tensor(f"l0_expS{i}", [128, 1024], BF16))
            for i in range(2)]
    expS_b = [Buf(), Buf()]
    expS2 = [ph.enter_context(nc.sbuf_tensor(f"l0_expT{i}", [128, 1024], BF16))
             for i in range(2)]
    expS2_b = [Buf(), Buf()]
    scr = [ph.enter_context(nc.sbuf_tensor("l0_scr0", [128, T], BF16))]
    scr_b = [Buf()]
    nrm = ph.enter_context(nc.sbuf_tensor("l0_nrm", [65, 1024], BF16))
    nrm_b = Buf()
    rec_row = nrm[64:65, :]
    rec_b = Buf()
    sumsq = ph.enter_context(nc.sbuf_tensor("l0_ssq", [16, T], BF16))
    sumsq_b = Buf()
    tiny = ph.enter_context(nc.sbuf_tensor("l0_tiny", [32, 16], F32))
    tiny_b = Buf()
    bias_sb = ph.enter_context(nc.sbuf_tensor("l0_bias", [128, 1], F32))
    bias_b = Buf()

    def proj_group(w, w_buf, mt, pp, rhs=None, rhs_bufs=None):
        rhs = rhs if rhs is not None else S.xT
        rhs_bufs = rhs_bufs if rhs_bufs is not None else S.xT_buf
        fns = []
        for kt in range(G):
            for c in range(NCH):
                fns.append(lambda e, kt=kt, c=c, w=w, mt=mt, pp=pp: e.matmul(
                    pp[:, c * 512:(c + 1) * 512],
                    w[:, kt, mt * 128:(mt + 1) * 128],
                    rhs[:, kt, c * 512:(c + 1) * 512],
                    start=(kt == 0), stop=(kt == G - 1)))
        pbufs = tuple(S.bank((0 if pp is S.pA else 4) + c)[1] for c in range(4))
        tok = ctx.op_multi("pe", fns,
                           reads=(w_buf,) + tuple(rhs_bufs) + (S.const_buf,),
                           writes=pbufs)
        return tok, pbufs

    def std_proj(w, w_buf, nmt, dst, dst_b):
        for mt in range(nmt):
            pp = S.pA if mt % 2 == 0 else S.pB
            tok, pbufs = proj_group(w, w_buf, mt, pp)
            ctx.op("ve", lambda e, dst=dst, mt=mt, pp=pp: e.tensor_copy(
                dst[:, mt, :], pp[:, 0:T]),
                reads=pbufs, writes=(dst_b[mt],))

    std_proj(wq, wq_b, 8, QT, QT_b)
    for mt in range(2):
        pp = S.pA if mt % 2 == 0 else S.pB
        tok, pbufs = proj_group(wk, wk_b, mt, pp)
        for hh in range(2):
            for half in range(2):
                ctx.op("ve", lambda e, mt=mt, hh=hh, half=half, pp=pp:
                       e.tensor_copy(
                    KTD[half * 64:half * 64 + 64, 2 * mt + hh, :],
                    pp[hh * 64:hh * 64 + 64, 0:T]),
                    reads=pbufs, writes=(KTD_b[2 * mt + hh],))

    ones_col = Vg[:].rearrange("p s (h c) -> p s h c", c=65)[:, :, :, 64:65]
    ctx.op("ve", lambda e: e.memset(ones_col, 1.0), writes=(Vg_b,))
    for st in range(NT):
        bi = 4 + st % 2
        bk, bkbuf = S.bank(bi)
        fns = [lambda e, st=st, kt=kt, bk=bk: e.matmul(
            bk[:, 0:256], S.xT[:, kt, st * 128:(st + 1) * 128],
            wv[:, kt, :], start=(kt == 0), stop=(kt == G - 1))
            for kt in range(G)]
        ctx.op_multi("pe", fns, reads=(wv_b,) + tuple(S.xT_buf),
                     writes=(bkbuf,))
        vgv = Vg[:].rearrange("p s (h c) -> p s h c", c=65)
        ctx.op("ve", lambda e, st=st, bk=bk, vgv=vgv: e.tensor_copy(
            vgv[:, st, :, 0:64],
            bk[:, 0:256].rearrange("p (h c) -> p h c", c=64)),
            reads=(bkbuf,), writes=(Vg_b,))

    def rope(dst, dst_b, g):
        pbufs = [S.bank(c)[1] for c in range(NCH)]
        fns = [lambda e, c=c: e.matmul(
            S.pA[:, c * 512:(c + 1) * 512], S.rot[:],
            dst[:, g, c * 512:(c + 1) * 512], start=True, stop=True)
            for c in range(NCH)]
        ctx.op_multi("pe", fns, reads=(dst_b[g], S.const_buf),
                     writes=tuple(pbufs))
        m1, m1_b = scr[0], scr_b[0]
        ctx.op("ve", lambda e: e.tensor_tensor(m1[:], dst[:, g, :], cos_ap,
                                               ALU.mult),
               reads=(dst_b[g], tab_b), writes=(m1_b,))
        ctx.op("ve", lambda e: e.tensor_tensor(dst[:, g, :], S.pA[:, 0:T],
                                               sin_ap, ALU.mult),
               reads=tuple(pbufs) + (tab_b,), writes=(dst_b[g],))
        ctx.op("ve", lambda e: e.tensor_tensor(dst[:, g, :], m1[:], dst[:, g, :],
                                               ALU.add),
               reads=(m1_b,), writes=(dst_b[g],))

    for g in range(8):
        rope(QT, QT_b, g)
    for g in range(4):
        rope(KTD, KTD_b, g)

    def head_sumsq(src, src_b, nptile, dst_row):
        for g in range(nptile):
            ctx.op("ve", lambda e, g=g: e.tensor_tensor(
                scr[0][:], src[:, g, :], src[:, g, :], ALU.mult),
                reads=(src_b[g],), writes=(scr_b[0],))
            fns = [lambda e, g=g, c=c: e.matmul(
                S.pA[0:16, c * 512:(c + 1) * 512],
                S.ind[:, g * 16:(g + 1) * 16],
                scr[0][:, c * 512:(c + 1) * 512],
                start=(g == 0), stop=(g == nptile - 1)) for c in range(NCH)]
            ctx.op_multi("pe", fns, reads=(scr_b[0], S.const_buf),
                         writes=tuple(S.bank(c)[1] for c in range(4)))
        pa_bufs = [S.bank(c)[1] for c in range(4)]
        ctx.op("ve", lambda e: e.tensor_copy(sumsq[:, :], S.pA[0:16, 0:T]),
               reads=tuple(pa_bufs), writes=(sumsq_b,))
        ctx.op("ve", lambda e: e.tensor_reduce(
            tiny[0:16, 0:1], sumsq[:, :], AX.X, ALU.max),
            reads=(sumsq_b,), writes=(tiny_b,))
        bk, bkbuf = S.bank(5)
        ctx.op("pe", lambda e, bk=bk: e.transpose(bk[0:1, 0:16], tiny[0:16, 0:1],
                                                  S.eye_f32[0:16, 0:16]),
               reads=(tiny_b, S.const_buf), writes=(bkbuf,))
        ctx.op("ve", lambda e: e.tensor_copy(tiny[0:1, 0:16], bk[0:1, 0:16]),
               reads=(bkbuf,), writes=(tiny_b,))
        ctx.op("ve", lambda e, dst_row=dst_row: e.tensor_reduce(
            tiny[0:1, dst_row:dst_row + 1], tiny[0:1, 0:16], AX.X, ALU.max),
            reads=(tiny_b,), writes=(tiny_b,))

    src_wo = S.dram["L0_attn_Wo"].rearrange("(kt p) m -> p kt m", p=128)
    ctx.op("gp", lambda e: e.dma_start(out=wo[:], in_=src_wo), writes=(wo_b,), q="qgp")

    head_sumsq(QT, QT_b, 8, 4)
    head_sumsq(KTD, KTD_b, 4, 5)
    ctx.op("ve", lambda e: e.tensor_tensor(
        tiny[0:1, 6:7], tiny[0:1, 4:5], tiny[0:1, 5:6], ALU.mult),
        reads=(tiny_b,), writes=(tiny_b,))
    ctx.op("act", lambda e: e.activation(tiny[0:1, 6:7], tiny[0:1, 6:7], AF.Ln),
           reads=(tiny_b,), writes=(tiny_b,))
    ctx.op("act", lambda e: e.activation(tiny[0:1, 7:8], tiny[0:1, 6:7], AF.Exp,
                                         scale=0.5),
           reads=(tiny_b,), writes=(tiny_b,))
    ctx.op("ve", lambda e: e.tensor_scalar(
        rec_row[0:1, 0:1], tiny[0:1, 7:8], -0.125, None, ALU.mult, ALU.bypass),
        reads=(tiny_b,), writes=(rec_b,))
    bk6, bk6buf = S.bank(6)
    ctx.op("pe", lambda e: e.matmul(bk6[:, 0:1], S.onesrow[64:65, :], rec_row[0:1, 0:1],
                                    start=True, stop=True),
           reads=(rec_b, S.const_buf), writes=(bk6buf,))
    ctx.op("ve", lambda e: e.tensor_copy(bias_sb[:], bk6[:, 0:1]),
           reads=(bk6buf,), writes=(bias_b,))

    expS4 = [expS[0], expS[1], expS2[0], expS2[1]]
    expS4_b = [expS_b[0], expS_b[1], expS2_b[0], expS2_b[1]]
    for h in range(16):
        g = h // 4
        qp, qr = h // 2, (h % 2) * 64

        def dot_group(a_, st):
            d0 = a_ * 2
            q0 = a_ * 1024
            dbufs = (S.bank(d0)[1], S.bank(d0 + 1)[1])
            fns = [lambda e, j=j, st=st, d0=d0, q0=q0: e.matmul(
                S.pA[:, (d0 + j) * 512:(d0 + j + 1) * 512],
                KTD[qr:qr + 64, g, st * 128:(st + 1) * 128],
                QT[qr:qr + 64, qp, q0 + j * 512:q0 + (j + 1) * 512],
                start=True, stop=True) for j in range(2)]
            ctx.op_multi("pe", fns, reads=(KTD_b[g], QT_b[qp]), writes=dbufs)
            return dbufs

        def av_group(a_, st):
            es, es_b = expS4[a_ * 2 + st % 2], expS4_b[a_ * 2 + st % 2]
            fns = [lambda e, j=j, st=st, es=es, a_=a_: e.matmul(
                S.pB[0:65, (a_ * 2 + j) * 512:(a_ * 2 + j + 1) * 512],
                Vg[:, st, g * 65:(g + 1) * 65],
                es[:, j * 512:(j + 1) * 512],
                start=(st == 0), stop=(st == NT - 1)) for j in range(2)]
            ctx.op_multi("pe", fns, reads=(es_b, Vg_b),
                         writes=(S.bank(4 + a_ * 2)[1], S.bank(5 + a_ * 2)[1]))

        def exp_step(a_, st, dbufs):
            d0 = a_ * 2
            dot = S.pA[:, d0 * 512:(d0 + 2) * 512]
            es, es_b = expS4[a_ * 2 + st % 2], expS4_b[a_ * 2 + st % 2]
            ctx.op("act", lambda e, es=es, dot=dot: e.activation(
                es[:], dot, AF.Exp, bias=bias_sb[:], scale=0.125),
                reads=tuple(dbufs) + (bias_b,), writes=(es_b,))

        for st in range(NT):
            dbA = dot_group(0, st)
            dbB = dot_group(1, st)
            exp_step(0, st, dbA)
            if st > 0:
                av_group(0, st - 1)
                av_group(1, st - 1)
            exp_step(1, st, dbB)
            ctx.op_multi("pe", [lambda e, j=j: e.matmul(
                S.pA[:, j * 512:(j + 1) * 512], S.rot[:],
                QT[:, 0, j * 512:(j + 1) * 512], start=True, stop=True)
                for j in range(2)],
                reads=(S.const_buf, QT_b[0]),
                writes=(S.bank(0)[1], S.bank(1)[1]))
        av_group(0, NT - 1)
        av_group(1, NT - 1)

        for a_ in range(2):
            av_bufs = (S.bank(4 + a_ * 2)[1], S.bank(5 + a_ * 2)[1])
            q0 = a_ * 1024
            ctx.op("act", lambda e, a_=a_: e.activation(
                rec_row[0:1, :], S.pB[64:65, a_ * 1024:(a_ + 1) * 1024], AF.Ln),
                reads=av_bufs, writes=(rec_b,))
            ctx.op("act", lambda e: e.activation(
                rec_row[0:1, :], rec_row[0:1, :], AF.Exp, scale=-1.0),
                reads=(rec_b,), writes=(rec_b,))
            bcb = (S.bank(2 * a_)[1], S.bank(2 * a_ + 1)[1])
            ctx.op_multi("pe", [lambda e, j=j, a_=a_: e.matmul(
                S.pA[0:64, (2 * a_ + j) * 512:(2 * a_ + j + 1) * 512],
                S.onesrow[64:65, 0:64],
                rec_row[0:1, j * 512:(j + 1) * 512], start=True, stop=True)
                for j in range(2)],
                reads=(rec_b, S.const_buf), writes=bcb)
            ctx.op("ve", lambda e, a_=a_: e.tensor_copy(
                nrm[0:64, :], S.pA[0:64, 2 * a_ * 512:(2 * a_ + 2) * 512]),
                reads=bcb, writes=(nrm_b,))
            wr = (AO_b[qp], tab_b) if qp < 4 else (AO_b[qp],)
            ctx.op("ve", lambda e, q0=q0, qr=qr, qp=qp, a_=a_: e.tensor_tensor(
                AO[qr:qr + 64, qp, q0:q0 + 1024],
                S.pB[0:64, a_ * 1024:(a_ + 1) * 1024],
                nrm[0:64, :], ALU.mult),
                reads=(nrm_b,) + av_bufs, writes=wr)

    r, r_b = QT, QT_b
    for mt in range(8):
        pp = S.pA if mt % 2 == 0 else S.pB
        tok, pbufs = proj_group(wo, wo_b, mt, pp, rhs=AO, rhs_bufs=AO_b)
        ctx.op("ve", lambda e, mt=mt, pp=pp: e.tensor_tensor(
            r[:, mt, :], pp[:, 0:T], S.xT[:, mt, :], ALU.add),
            reads=pbufs + (S.xT_buf[mt],), writes=(r_b[mt],))

    emit_ln(S, r, r_b, S.xT, S.xT_buf, scr[0], scr_b[0])


PHASES["l0a"] = emit_l0a


def emit_router(S, wg_name, bg_name, temp, g8, g8_b, sigscale):
    ctx, nc = S.ctx, S.nc
    ph = S.stack_phase
    tg = wg_name
    wg, wg_b = load_weight_2d(S, wg_name, 8, 1, mt_w=8, tag=wg_name + "_r")
    lg = ph.enter_context(nc.sbuf_tensor(tg + "rt_lg", [128, 128], F32))
    lg_b = Buf()
    t1 = ph.enter_context(nc.sbuf_tensor(tg + "rt_t1", [128, 128], F32))
    t1_b = Buf()
    t2 = ph.enter_context(nc.sbuf_tensor(tg + "rt_t2", [128, 128], F32))
    t2_b = Buf()
    mm_ = ph.enter_context(nc.sbuf_tensor(tg + "rt_mm", [128, 64], F32))
    mm_b = Buf()
    fns = []
    for kt in range(G):
        for c in range(NCH):
            fns.append(lambda e, c=c, kt=kt: e.matmul(
                S.pA[0:8, c * 512:(c + 1) * 512], wg[:, kt, :],
                S.xT[:, kt, c * 512:(c + 1) * 512],
                start=(kt == 0), stop=(kt == G - 1)))
    ctx.op_multi("pe", fns, reads=(wg_b,) + tuple(S.xT_buf) + (S.const_buf,),
                 writes=tuple(S.bank(c)[1] for c in range(4)))
    lgT = ph.enter_context(nc.sbuf_tensor(tg + "rt_lgT", [8, T], F32))
    lgT_b = Buf()
    pa_bufs0 = [S.bank(c)[1] for c in range(4)]
    ctx.op("ve", lambda e: e.tensor_copy(lgT[:], S.pA[0:8, 0:T]),
           reads=tuple(pa_bufs0), writes=(lgT_b,))
    if bg_name is not None:
        bg = ph.enter_context(nc.sbuf_tensor(tg + "rt_bg", [8, 1], F32))
        bg_b = Buf()
        ctx.op("gp", lambda e: e.dma_start(out=bg[:], in_=S.dram[bg_name].unsqueeze(1)),
               writes=(bg_b,), q="qgp")
        ctx.op("ve", lambda e: e.tensor_tensor(
            lgT[:], lgT[:], bg[:].broadcast_to([8, T]), ALU.add),
            reads=(bg_b,), writes=(lgT_b,))
    b4, b4buf = S.bank(4)
    ctx.op_multi("pe", [lambda e, t=t, b4=b4: e.transpose(
        b4[:, 8 * t:8 * t + 8], lgT[:, t * 128:(t + 1) * 128],
        S.eye_f32[0:8, 0:8]) for t in range(NT)],
        reads=(lgT_b, S.const_buf), writes=(b4buf,))
    ctx.op("ve", lambda e, b4=b4: e.tensor_copy(lg[:], b4[:, 0:128]),
           reads=(b4buf,), writes=(lg_b,))
    lgv = lg[:].rearrange("p (t e) -> p t e", e=8)
    m1 = mm_[:, 0:16]
    m2 = mm_[:, 16:32]
    w2 = mm_[:, 32:48]
    w1 = mm_[:, 48:64]
    m1b = m1.unsqueeze(2).broadcast_to([128, 16, 8])
    m2b = m2.unsqueeze(2).broadcast_to([128, 16, 8])
    w1b = w1.unsqueeze(2).broadcast_to([128, 16, 8])
    w2b = w2.unsqueeze(2).broadcast_to([128, 16, 8])
    t1v = t1[:].rearrange("p (t e) -> p t e", e=8)
    t2v = t2[:].rearrange("p (t e) -> p t e", e=8)
    ctx.op("ve", lambda e: e.tensor_reduce(m1, lgv, AX.X, ALU.max),
           reads=(lg_b,), writes=(mm_b,))
    ctx.op("ve", lambda e: e.tensor_tensor(t1v, lgv, m1b, ALU.is_equal),
           reads=(lg_b, mm_b), writes=(t1_b,))
    ctx.op("ve", lambda e: e.scalar_tensor_tensor(
        t2v, t1v, -1e30, lgv, ALU.mult, ALU.add),
        reads=(t1_b, lg_b), writes=(t2_b,))
    ctx.op("ve", lambda e: e.tensor_reduce(m2, t2v, AX.X, ALU.max),
           reads=(t2_b,), writes=(mm_b,))
    ctx.op("ve", lambda e: e.tensor_tensor(mm_[:, 32:48], m2, m1, ALU.subtract),
           reads=(mm_b,), writes=(mm_b,))
    ctx.op("act", lambda e: e.activation(w2, mm_[:, 32:48], AF.Sigmoid,
                                         scale=sigscale),
           reads=(mm_b,), writes=(mm_b,))
    ctx.op("ve", lambda e: e.tensor_scalar(w1, w2, -1.0, 1.0, ALU.mult, ALU.add),
           reads=(mm_b,), writes=(mm_b,))
    ctx.op("ve", lambda e: e.tensor_tensor(t2v, lgv, m2b, ALU.is_equal),
           reads=(lg_b, mm_b), writes=(t2_b,))
    ctx.op("ve", lambda e: e.tensor_tensor(t1v, t1v, w1b, ALU.mult),
           reads=(t1_b, mm_b), writes=(t1_b,))
    ctx.op("ve", lambda e: e.tensor_tensor(t2v, t2v, w2b, ALU.mult),
           reads=(t2_b, mm_b), writes=(t2_b,))
    ctx.op("ve", lambda e: e.tensor_tensor(t1[:], t1[:], t2[:], ALU.add),
           reads=(t1_b, t2_b), writes=(t1_b,))
    ctx.op_multi("pe", [lambda e, t=t: e.transpose(
        S.pA[0:8, t * 128:(t + 1) * 128], t1[:, 8 * t:8 * t + 8],
        S.eye_f32[:]) for t in range(NT)],
        reads=(t1_b, S.const_buf),
        writes=tuple(S.bank(c)[1] for c in range(4)))
    pa_bufs = [S.bank(c)[1] for c in range(4)]
    ctx.op("ve", lambda e: e.tensor_copy(g8[0:8, :], S.pA[0:8, 0:T]),
           reads=tuple(pa_bufs), writes=(g8_b,))


def emit_l0f(S):
    ctx, nc = S.ctx, S.nc
    ph = S.stack_phase
    g8 = ph.enter_context(nc.sbuf_tensor("f0_g8", [8, T], BF16))
    g8_b = Buf()
    S.sele = ph.enter_context(nc.sbuf_tensor("sele_l", [8, 8 * 128], BF16))
    ctx.op("gp", lambda e: e.dma_start(out=S.sele[:], in_=S.dram["C_sele"]),
           writes=(S.const_buf,), q="qgp")
    emit_router(S, "L0_ffn_gate_w", None, 1.0, g8, g8_b, 1.0)

    yacc = ph.enter_context(nc.sbuf_tensor("f0_yacc", [128, G, T], BF16))
    yacc_b = [Buf() for _ in range(G)]
    hid = ph.enter_context(nc.sbuf_tensor("f0_hid", [128, 8, T], BF16))
    hid_b = [Buf() for _ in range(8)]
    wdn = ph.enter_context(nc.sbuf_tensor("f0_wdn", [128, 16, 1024], BF16))
    wdn_b = Buf()
    NSLOT = 4
    wgu = [ph.enter_context(nc.sbuf_tensor(f"f0_wgu{i}", [128, 8, 256], BF16))
           for i in range(NSLOT)]
    wgu_b = [Buf() for _ in range(NSLOT)]
    gbc = [ph.enter_context(nc.sbuf_tensor(f"f0_gbc{i}", [128, T], BF16))
           for i in range(2)]
    gbc_b = [Buf(), Buf()]
    tsc = ph.enter_context(nc.sbuf_tensor("f0_tsc", [128, T], BF16))
    tsc_b = Buf()

    gup = S.dram["L0_ffn_gate_up_w"]
    dwn = S.dram["L0_ffn_down_w"]
    FF = 2048

    def dma_pair(e_i, i_g):
        slot = i_g % NSLOT
        w, w_b = wgu[slot], wgu_b[slot]
        ctx.op("gp", lambda e, e_i=e_i, i_g=i_g, w=w: e.dma_start(
            out=w[:, :, 0:128],
            in_=gup[e_i, :, i_g * 128:(i_g + 1) * 128]
            .rearrange("(kt p) m -> p kt m", p=128)),
            writes=(w_b,), q="qgp")
        ctx.op("gp", lambda e, e_i=e_i, i_g=i_g, w=w: e.dma_start(
            out=w[:, :, 128:256],
            in_=gup[e_i, :, FF + i_g * 128:FF + (i_g + 1) * 128]
            .rearrange("(kt p) m -> p kt m", p=128)),
            writes=(w_b,), q="qgp")

    for e_i in range(8):
        dma_pair(e_i, 0)
        dma_pair(e_i, 1)
        ctx.op("gp", lambda e, e_i=e_i: e.dma_start(
            out=wdn[:], in_=dwn[e_i].rearrange("(kt p) m -> p kt m", p=128)),
            writes=(wdn_b,), q="qgp")
        gb, gb_buf = gbc[e_i % 2], gbc_b[e_i % 2]
        bk, bkbuf = S.bank(4)
        ctx.op_multi("pe", [lambda e, e_i=e_i, c=c: e.matmul(
            S.pB[:, c * 512:(c + 1) * 512],
            S.sele[:, e_i * 128:(e_i + 1) * 128],
            g8[0:8, c * 512:(c + 1) * 512], start=True, stop=True)
            for c in range(NCH)],
            reads=(g8_b, S.const_buf),
            writes=tuple(S.bank(4 + c)[1] for c in range(4)))
        ctx.op("ve", lambda e, gb=gb: e.tensor_copy(gb[:], S.pB[:, 0:T]),
               reads=tuple(S.bank(4 + c)[1] for c in range(4)),
               writes=(gb_buf,))
        for s in range(2):
            for j in range(8):
                i_g = s * 8 + j
                slot = (s * 8 + j) % NSLOT
                w, w_b = wgu[slot], wgu_b[slot]
                if i_g >= 2:
                    dma_pair(e_i, i_g)
                pa_bufs = tuple(S.bank(c)[1] for c in range(4))
                pb_bufs = tuple(S.bank(4 + c)[1] for c in range(4))
                fns = []
                for kt in range(G):
                    for c in range(NCH):
                        fns.append(lambda e, c=c, kt=kt, w=w: e.matmul(
                            S.pA[:, c * 512:(c + 1) * 512], w[:, kt, 0:128],
                            S.xT[:, kt, c * 512:(c + 1) * 512],
                            start=(kt == 0), stop=(kt == G - 1)))
                ctx.op_multi("pe", fns, reads=(w_b,) + tuple(S.xT_buf),
                             writes=pa_bufs)
                fns = []
                for kt in range(G):
                    for c in range(NCH):
                        fns.append(lambda e, c=c, kt=kt, w=w: e.matmul(
                            S.pB[:, c * 512:(c + 1) * 512], w[:, kt, 128:256],
                            S.xT[:, kt, c * 512:(c + 1) * 512],
                            start=(kt == 0), stop=(kt == G - 1)))
                ctx.op_multi("pe", fns, reads=(w_b,) + tuple(S.xT_buf),
                             writes=pb_bufs)
                ctx.op("act", lambda e: e.activation(tsc[:], S.pA[:, 0:T],
                                                     AF.Silu),
                       reads=pa_bufs, writes=(tsc_b,))
                ctx.op("ve", lambda e, j=j, gb=gb: e.tensor_tensor(
                    hid[:, j, :], S.pB[:, 0:T], gb[:], ALU.mult),
                    reads=pb_bufs + (gb_buf,), writes=(hid_b[j],))
                ctx.op("ve", lambda e, j=j: e.tensor_tensor(
                    hid[:, j, :], hid[:, j, :], tsc[:], ALU.mult),
                    reads=(tsc_b,), writes=(hid_b[j],))
            for m in range(G):
                pp = S.pA if m % 2 == 0 else S.pB
                poff = 0 if m % 2 == 0 else 4
                pbufs = tuple(S.bank(poff + c)[1] for c in range(4))
                fns = []
                for j in range(G):
                    kt = s * 8 + j
                    for c in range(NCH):
                        fns.append(lambda e, m=m, c=c, j=j, kt=kt, pp=pp:
                                   e.matmul(
                            pp[:, c * 512:(c + 1) * 512],
                            wdn[:, kt, m * 128:(m + 1) * 128],
                            hid[:, j, c * 512:(c + 1) * 512],
                            start=(j == 0), stop=(j == G - 1)))
                ctx.op_multi("pe", fns,
                             reads=(wdn_b,) + tuple(hid_b), writes=pbufs)
                if e_i == 0 and s == 0:
                    ctx.op("ve", lambda e, m=m, pp=pp: e.tensor_copy(
                        yacc[:, m, :], pp[:, 0:T]),
                        reads=pbufs, writes=(yacc_b[m],))
                else:
                    ctx.op("ve", lambda e, m=m, pp=pp: e.tensor_tensor(
                        yacc[:, m, :], pp[:, 0:T], yacc[:, m, :], ALU.add),
                        reads=pbufs, writes=(yacc_b[m],))

    for g in range(G):
        ctx.op("ve", lambda e, g=g: e.tensor_tensor(
            yacc[:, g, :], yacc[:, g, :], S.xT[:, g, :], ALU.add),
            reads=(S.xT_buf[g],), writes=(yacc_b[g],))
    emit_ln(S, yacc, yacc_b, S.xT, S.xT_buf, tsc, tsc_b)


PHASES["l0f"] = emit_l0f




L1A_LIMIT = 99
L1A_YAT = True
L1A_NORM = True


def emit_l1a(S):
    ctx, nc = S.ctx, S.nc
    ph = S.stack_phase

    QT = ph.enter_context(nc.sbuf_tensor("l1_QT", [128, 8, T], BF16))
    QT_b = [Buf() for _ in range(8)]
    KT = ph.enter_context(nc.sbuf_tensor("l1_KT", [128, 8, T], BF16))
    KT_b = [Buf() for _ in range(8)]
    Vg = ph.enter_context(nc.sbuf_tensor("l1_Vg", [128, NT, 16 * 65], BF16))
    Vg_b = Buf()
    W = [ph.enter_context(nc.sbuf_tensor(f"l1_W{i}", [128, 8, 1024], BF16))
         for i in range(2)]
    W_b = [Buf(), Buf()]
    sT = [ph.enter_context(nc.sbuf_tensor(f"l1_sT{i}", [128, 1024], BF16))
          for i in range(3)]
    sT_b = [Buf(), Buf(), Buf()]
    Lb = [ph.enter_context(nc.sbuf_tensor("l1_Lb0", [128, 1024], F32))]
    Lb_b = [Buf()]
    Rb = [ph.enter_context(nc.sbuf_tensor(f"l1_Rb{i}", [128, 1024], BF16))
          for i in range(2)]
    Rb_b = [Buf(), Buf()]
    scr = ph.enter_context(nc.sbuf_tensor("l1_scr", [128, T], BF16))
    scr_b = Buf()
    nrm = ph.enter_context(nc.sbuf_tensor("l1_nrm", [65, 1024], BF16))
    nrm_b = Buf()
    rec_row = nrm[64:65, :]
    rec_b = Buf()
    bstage = ph.enter_context(nc.sbuf_tensor("l1_bst", [32, 128], BF16))
    bstage_b = Buf()
    bcm = ph.enter_context(nc.sbuf_tensor("l1_bcm", [128, 32], F32))
    bcm_b = Buf()
    bvrow = None
    bvrow_b = Buf()
    ssq = [Lb[0][:][0:16, :].bitcast(BF16)]
    ssq_b = [Lb_b[0]]
    AO, AO_b = QT, QT_b

    for i, bn in enumerate(["L1_attn_bq", "L1_attn_bk", "L1_attn_bv",
                            "L1_attn_bo"]):
        ctx.op("gp", lambda e, i=i, bn=bn: e.dma_start(
            out=bstage[8 * i:8 * i + 8, :],
            in_=S.dram[bn].rearrange("(g p) -> g p", p=128)),
            writes=(bstage_b,), q="qgp")
    bk7, bk7buf = S.bank(7)
    ctx.op("pe", lambda e, bk7=bk7: e.transpose(
        bk7[:, 0:32].bitcast(BF16)[:, 0:32], bstage[:], S.eye_bf[0:32, 0:32]),
        reads=(bstage_b, S.const_buf), writes=(bk7buf,))
    ctx.op("ve", lambda e, bk7=bk7: e.tensor_copy(
        bcm[:], bk7[:, 0:32].bitcast(BF16)[:, 0:32]),
        reads=(bk7buf,), writes=(bcm_b,))
    ctx.op("gp", lambda e: e.dma_start(
        out=nrm[0:1, :], in_=S.dram["L1_attn_bv"].unsqueeze(0)),
        writes=(bvrow_b,), q="qgp")

    def qk_proj(wname, wslot, dst, dst_b, bias_off):
        w, w_b = W[wslot], W_b[wslot]
        ctx.op("gp", lambda e, wname=wname, w=w: e.dma_start(
            out=w[:], in_=S.dram[wname].rearrange("(kt p) m -> p kt m", p=128)),
            writes=(w_b,), q="qgp")
        for mt in range(8):
            pp = S.pA if mt % 2 == 0 else S.pB
            poff = 0 if mt % 2 == 0 else 4
            pbufs = tuple(S.bank(poff + c)[1] for c in range(4))
            fns = []
            for kt in range(G):
                for c in range(NCH):
                    fns.append(lambda e, w=w, mt=mt, c=c, kt=kt, pp=pp:
                               e.matmul(
                        pp[:, c * 512:(c + 1) * 512],
                        w[:, kt, mt * 128:(mt + 1) * 128],
                        S.xT[:, kt, c * 512:(c + 1) * 512],
                        start=(kt == 0), stop=(kt == G - 1)))
            ctx.op_multi("pe", fns, reads=(w_b,) + tuple(S.xT_buf),
                         writes=pbufs)
            ctx.op("ve", lambda e, dst=dst, mt=mt, pp=pp, bias_off=bias_off:
                   e.tensor_scalar(
                dst[:, mt, :], pp[:, 0:T],
                bcm[:, bias_off + mt:bias_off + mt + 1], None,
                ALU.add, ALU.bypass),
                reads=pbufs + (bcm_b,), writes=(dst_b[mt],))

    qk_proj("L1_attn_Wq", 0, QT, QT_b, 0)
    qk_proj("L1_attn_Wk", 1, KT, KT_b, 8)
    if L1A_LIMIT < 2:
        return

    wv, wv_b = W[0], W_b[0]
    ctx.op("gp", lambda e: e.dma_start(
        out=wv[:], in_=S.dram["L1_attn_Wv"].rearrange("(kt p) m -> p kt m",
                                                      p=128)),
        writes=(wv_b,), q="qgp")
    ones_col = Vg[:].rearrange("p s (h c) -> p s h c", c=65)[:, :, :, 64:65]
    ctx.op("ve", lambda e: e.memset(ones_col, 1.0), writes=(Vg_b,))
    vgv = Vg[:].rearrange("p s (h c) -> p s h c", c=65)
    for st in range(NT):
        vbufs = (S.bank(4)[1], S.bank(5)[1])
        fns = []
        for kt in range(G):
            for j in range(2):
                fns.append(lambda e, st=st, kt=kt, j=j: e.matmul(
                    S.pB[:, j * 512:(j + 1) * 512],
                    S.xT[:, kt, st * 128:(st + 1) * 128],
                    wv[:, kt, j * 512:(j + 1) * 512],
                    start=(kt == 0), stop=False))
        for j in range(2):
            fns.append(lambda e, st=st, j=j: e.matmul(
                S.pB[:, j * 512:(j + 1) * 512], S.onesrow[0:1, 0:128],
                nrm[0:1, j * 512:(j + 1) * 512],
                start=False, stop=True))
        ctx.op_multi("pe", fns,
                     reads=(wv_b, bvrow_b, S.const_buf) + tuple(S.xT_buf),
                     writes=vbufs)
        ctx.op("ve", lambda e, st=st, vgv=vgv: e.tensor_copy(
            vgv[:, st, :, 0:64],
            S.pB[:, 0:1024].rearrange("p (h c) -> p h c", c=64)),
            reads=vbufs, writes=(Vg_b,))

    if L1A_LIMIT < 3:
        return

    def normalize(dst, dst_b, ssq_i):
        sq, sq_b = ssq[ssq_i], ssq_b[ssq_i]
        for g in range(G):
            ctx.op("ve", lambda e, g=g: e.tensor_tensor(
                scr[:], dst[:, g, :], dst[:, g, :], ALU.mult),
                reads=(dst_b[g],), writes=(scr_b,))
            ctx.op_multi("pe", [lambda e, g=g, c=c: e.matmul(
                S.pA[0:16, c * 512:(c + 1) * 512],
                S.ind[:, g * 16:(g + 1) * 16],
                scr[:, c * 512:(c + 1) * 512],
                start=(g == 0), stop=(g == G - 1)) for c in range(NCH)],
                reads=(scr_b, S.const_buf),
                writes=tuple(S.bank(c)[1] for c in range(4)))
        pa_bufs = [S.bank(c)[1] for c in range(4)]
        ctx.op("ve", lambda e, sq=sq: e.tensor_copy(sq[:], S.pA[0:16, 0:T]),
               reads=tuple(pa_bufs), writes=(sq_b,))
        ctx.op("act", lambda e, sq=sq: e.activation(sq[:], sq[:], AF.Ln),
               reads=(sq_b,), writes=(sq_b,))
        ctx.op("act", lambda e, sq=sq: e.activation(sq[:], sq[:], AF.Exp,
                                                    scale=0.5),
               reads=(sq_b,), writes=(sq_b,))
        ctx.op("ve", lambda e, sq=sq: e.tensor_scalar(
            sq[:], sq[:], 1e-8, None, ALU.add, ALU.bypass),
            reads=(sq_b,), writes=(sq_b,))
        ctx.op("act", lambda e, sq=sq: e.activation(sq[:], sq[:], AF.Ln),
               reads=(sq_b,), writes=(sq_b,))
        ctx.op("act", lambda e, sq=sq: e.activation(sq[:], sq[:], AF.Exp,
                                                    scale=-1.0),
               reads=(sq_b,), writes=(sq_b,))
        for g in range(G):
            pbufs = tuple(S.bank(4 + c)[1] for c in range(4))
            ctx.op_multi("pe", [lambda e, g=g, c=c, sq=sq: e.matmul(
                S.pB[:, c * 512:(c + 1) * 512],
                S.sel2[:, g * 128:(g + 1) * 128],
                sq[0:16, c * 512:(c + 1) * 512], start=True, stop=True)
                for c in range(NCH)],
                reads=(sq_b, S.const_buf), writes=pbufs)
            ctx.op("ve", lambda e, g=g: e.tensor_tensor(
                dst[:, g, :], dst[:, g, :], S.pB[:, 0:T], ALU.mult),
                reads=pbufs, writes=(dst_b[g],))

    normalize(QT, QT_b, 0)
    normalize(KT, KT_b, 0)

    if L1A_LIMIT < 4:
        return

    for h in range(16):
        qp, qr = h // 2, (h % 2) * 64
        for qh in range(2):
            q0 = qh * 1024
            av_bufs = [S.bank(4)[1], S.bank(5)[1]]
            def dap(st, j):
                pr = st % 3
                if pr < 2:
                    return S.pA[:, (pr * 2 + j) * 512:(pr * 2 + j + 1) * 512]
                return S.pB[:, (2 + j) * 512:(3 + j) * 512]

            def dbuf(st, j):
                return S.bank([0, 2, 6][st % 3] + j)[1]

            def dot_group(st):
                dbufs = (dbuf(st, 0), dbuf(st, 1))
                fns = [lambda e, j=j, st=st, qp=qp, qr=qr, q0=q0:
                       e.matmul(
                    dap(st, j),
                    KT[qr:qr + 64, qp, st * 128:(st + 1) * 128],
                    QT[qr:qr + 64, qp, q0 + j * 512:q0 + (j + 1) * 512],
                    start=True, stop=True) for j in range(2)]
                ctx.op_multi("pe", fns, reads=(KT_b[qp], QT_b[qp]),
                             writes=dbufs)
                return dbufs

            def av_group(st):
                sTt, sTt_b = sT[st % 3], sT_b[st % 3]
                fns = [lambda e, j=j, st=st, h=h, sTt=sTt: e.matmul(
                    S.pB[0:65, j * 512:(j + 1) * 512],
                    Vg[:, st, h * 65:(h + 1) * 65],
                    sTt[:, j * 512:(j + 1) * 512],
                    start=(st == 0), stop=(st == NT - 1)) for j in range(2)]
                ctx.op_multi("pe", fns, reads=(sTt_b, Vg_b),
                             writes=(av_bufs[0], av_bufs[1]))

            for st in range(NT):
                dbufs = dot_group(st)
                if st > 1:
                    av_group(st - 2)
                rbuf = st % 3
                sTt, sTt_b = sT[rbuf], sT_b[rbuf]
                Lt, Lt_b = Lb[0], Lb_b[0]
                Rt, Rt_b = Rb[st % 2], Rb_b[st % 2]
                ctx.op("act", lambda e, Lt=Lt, st=st: e.activation(
                    Lt[:, 0:512], dap(st, 0),
                    AF.Ln, bias=1.1, scale=-1.0),
                    reads=(dbufs[0], S.const_buf), writes=(Lt_b,))
                ctx.op("act", lambda e, Lt=Lt, st=st: e.activation(
                    Lt[:, 512:1024], dap(st, 1),
                    AF.Ln, bias=1.1, scale=-1.0),
                    reads=(dbufs[1], S.const_buf), writes=(Lt_b,))
                ctx.op("act", lambda e, Rt=Rt, Lt=Lt: e.activation(
                    Rt[:], Lt[:], AF.Exp, scale=-1.0),
                    reads=(Lt_b,), writes=(Rt_b,))
                ctx.op("ve", lambda e, sTt=sTt, Rt=Rt, st=st:
                       e.scalar_tensor_tensor(
                    sTt[:, 0:512], Rt[:, 0:512], 4.41, dap(st, 0),
                    ALU.mult, ALU.subtract),
                    reads=(Rt_b, dbufs[0]), writes=(sTt_b,))
                ctx.op("ve", lambda e, sTt=sTt, Rt=Rt, st=st:
                       e.scalar_tensor_tensor(
                    sTt[:, 512:1024], Rt[:, 512:1024], 4.41, dap(st, 1),
                    ALU.mult, ALU.subtract),
                    reads=(Rt_b, dbufs[1]), writes=(sTt_b,))
                ctx.op("ve", lambda e, sTt=sTt: e.tensor_scalar(
                    sTt[:], sTt[:], -3.1, None, ALU.add, ALU.bypass),
                    reads=(), writes=(sTt_b,))
            av_group(NT - 2)
            av_group(NT - 1)
            if L1A_NORM:
                ctx.op("act", lambda e: e.activation(
                    rec_row[0:1, :], S.pB[64:65, 0:1024], AF.Ln, bias=1e-6),
                    reads=tuple(av_bufs) + (S.const_buf,), writes=(rec_b,))
                ctx.op("act", lambda e: e.activation(
                    rec_row[0:1, :], rec_row[0:1, :], AF.Exp, scale=-1.0),
                    reads=(rec_b,), writes=(rec_b,))
            else:
                ctx.op("ve", lambda e: e.memset(rec_row[0:1, :], 0.001),
                       reads=tuple(av_bufs), writes=(rec_b,))
            for j in range(2):
                bk, bkbuf = S.bank(6 + j)
                ctx.op("pe", lambda e, j=j, bk=bk: e.matmul(
                    bk[0:64, :], S.onesrow[64:65, 0:64],
                    rec_row[0:1, j * 512:(j + 1) * 512], start=True, stop=True),
                    reads=(rec_b, S.const_buf), writes=(bkbuf,))
            b6, b6buf = S.bank(6)
            b7, b7buf = S.bank(7)
            ctx.op("ve", lambda e, b6=b6: e.tensor_copy(nrm[0:64, 0:512],
                                                        b6[0:64, :]),
                   reads=(b6buf,), writes=(nrm_b, bvrow_b))
            ctx.op("ve", lambda e, b7=b7: e.tensor_copy(nrm[0:64, 512:1024],
                                                        b7[0:64, :]),
                   reads=(b7buf,), writes=(nrm_b,))
            ctx.op("ve", lambda e, qr=qr, qp=qp, q0=q0: e.tensor_tensor(
                AO[qr:qr + 64, qp, q0:q0 + 1024], S.pB[0:64, 0:1024],
                nrm[0:64, :], ALU.mult),
                reads=(nrm_b,) + tuple(av_bufs), writes=(AO_b[qp],))

    if L1A_LIMIT < 5:
        return

    wo, wo_b = W[1], W_b[1]
    ctx.op("gp", lambda e: e.dma_start(
        out=wo[:], in_=S.dram["L1_attn_Wo"].rearrange("(kt p) m -> p kt m",
                                                      p=128)),
        writes=(wo_b,), q="qgp")
    r, r_b = KT, KT_b
    for mt in range(8):
        for c in range(NCH):
            bk, bkbuf = S.bank((mt * NCH + c) % 4)
            for kt in range(G):
                ctx.op("pe", lambda e, mt=mt, c=c, kt=kt, bk=bk: e.matmul(
                    bk, wo[:, kt, mt * 128:(mt + 1) * 128],
                    AO[:, kt, c * 512:(c + 1) * 512],
                    start=(kt == 0), stop=(kt == G - 1)),
                    reads=(wo_b, AO_b[kt]), writes=(bkbuf,))
            ctx.op("ve", lambda e, mt=mt, c=c, bk=bk: e.tensor_scalar(
                scr[:, c * 512:(c + 1) * 512], bk,
                bcm[:, 24 + mt:24 + mt + 1], None, ALU.add, ALU.bypass),
                reads=(bkbuf, bcm_b), writes=(scr_b,))
            ctx.op("ve", lambda e, mt=mt, c=c: e.tensor_tensor(
                r[:, mt, c * 512:(c + 1) * 512],
                scr[:, c * 512:(c + 1) * 512],
                S.xT[:, mt, c * 512:(c + 1) * 512], ALU.add),
                reads=(scr_b, S.xT_buf[mt]), writes=(r_b[mt],))

    emit_ln(S, r, r_b, S.xT, S.xT_buf, scr, scr_b)


PHASES["l1a"] = emit_l1a




def emit_l1f(S):
    ctx, nc = S.ctx, S.nc
    ph = S.stack_phase
    g8 = ph.enter_context(nc.sbuf_tensor("f1_g8", [8, T], BF16))
    g8_b = Buf()
    S.sele = ph.enter_context(nc.sbuf_tensor("sele_l1", [8, 8 * 128], BF16))
    ctx.op("gp", lambda e: e.dma_start(out=S.sele[:], in_=S.dram["C_sele"]),
           writes=(S.const_buf,), q="qgp")
    emit_router(S, "L1_ffn_Wg", "L1_ffn_bg", 5.0, g8, g8_b, 0.2)

    yacc = ph.enter_context(nc.sbuf_tensor("f1_yacc", [128, G, T], BF16))
    yacc_b = [Buf() for _ in range(G)]
    hid = ph.enter_context(nc.sbuf_tensor("f1_hid", [128, 8, T], BF16))
    hid_b = [Buf() for _ in range(8)]
    NS1 = 4
    w1s = [ph.enter_context(nc.sbuf_tensor(f"f1_w1{i}", [128, 8, 128], BF16))
           for i in range(NS1)]
    w1s_b = [Buf() for _ in range(NS1)]
    w2e = ph.enter_context(nc.sbuf_tensor("f1_w2e", [128, 16, 1024], BF16))
    w2e_b = Buf()
    gbc = [ph.enter_context(nc.sbuf_tensor(f"f1_gbc{i}", [128, T], BF16))
           for i in range(2)]
    gbc_b = [Buf(), Buf()]
    tsc = ph.enter_context(nc.sbuf_tensor("f1_tsc", [128, T], BF16))
    tsc_b = Buf()
    b1st = ph.enter_context(nc.sbuf_tensor("f1_b1st", [16, 128], F32))
    b1st_b = Buf()
    b1cm = ph.enter_context(nc.sbuf_tensor("f1_b1cm", [128, 8, 16], F32))
    b1cm_b = Buf()
    b2sb = ph.enter_context(nc.sbuf_tensor("f1_b2", [8, 1024], BF16))
    b2sb_b = Buf()
    ctx.op("gp", lambda e: e.dma_start(out=b2sb[:], in_=S.dram["L1_ffn_b2"]),
           writes=(b2sb_b,), q="qgp")
    for e_i in range(8):
        ctx.op("gp", lambda e, e_i=e_i: e.dma_start(
            out=b1st[:], in_=S.dram["L1_ffn_b1"][e_i].rearrange(
                "(a p) -> a p", p=128)),
            writes=(b1st_b,), q="qgp")
        bk7, bk7buf = S.bank(7)
        ctx.op("pe", lambda e, bk7=bk7: e.transpose(
            bk7[:, 0:16], b1st[:], S.eye_f32[0:16, 0:16]),
            reads=(b1st_b, S.const_buf), writes=(bk7buf,))
        ctx.op("ve", lambda e, e_i=e_i, bk7=bk7: e.tensor_copy(
            b1cm[:, e_i, :], bk7[:, 0:16]),
            reads=(bk7buf,), writes=(b1cm_b,))

    W1 = S.dram["L1_ffn_W1"]
    W2 = S.dram["L1_ffn_W2"]
    HD = 2048

    def dma_w1(e_i, i_g):
        slot = i_g % NS1
        w, w_b = w1s[slot], w1s_b[slot]
        ctx.op("gp", lambda e, e_i=e_i, i_g=i_g, w=w: e.dma_start(
            out=w[:], in_=W1[e_i, :, i_g * 128:(i_g + 1) * 128]
            .rearrange("(kt p) m -> p kt m", p=128)),
            writes=(w_b,), q="qgp")

    for e_i in range(8):
        dma_w1(e_i, 0)
        dma_w1(e_i, 1)
        ctx.op("gp", lambda e, e_i=e_i: e.dma_start(
            out=w2e[:, 0:8, :],
            in_=W2[e_i, 0:1024, :].rearrange("(kt p) m -> p kt m", p=128)),
            writes=(w2e_b,), q="qgp")
        dma_w1(e_i, 2)
        dma_w1(e_i, 3)
        ctx.op("gp", lambda e, e_i=e_i: e.dma_start(
            out=w2e[:, 8:16, :],
            in_=W2[e_i, 1024:2048, :].rearrange("(kt p) m -> p kt m", p=128)),
            writes=(w2e_b,), q="qgp")
        gb, gb_buf = gbc[e_i % 2], gbc_b[e_i % 2]
        ctx.op_multi("pe", [lambda e, e_i=e_i, c=c: e.matmul(
            S.pB[:, c * 512:(c + 1) * 512],
            S.sele[:, e_i * 128:(e_i + 1) * 128],
            g8[0:8, c * 512:(c + 1) * 512], start=True, stop=True)
            for c in range(NCH)],
            reads=(g8_b, S.const_buf),
            writes=tuple(S.bank(4 + c)[1] for c in range(4)))
        ctx.op("ve", lambda e, gb=gb: e.tensor_copy(gb[:], S.pB[:, 0:T]),
               reads=tuple(S.bank(4 + c)[1] for c in range(4)),
               writes=(gb_buf,))
        for s in range(2):
            for j in range(8):
                i_g = s * 8 + j
                slot = i_g % NS1
                w, w_b = w1s[slot], w1s_b[slot]
                if i_g >= 4:
                    dma_w1(e_i, i_g)
                pp = S.pA if j % 2 == 0 else S.pB
                poff = 0 if j % 2 == 0 else 4
                pbufs = tuple(S.bank(poff + c)[1] for c in range(4))
                fns = []
                for kt in range(G):
                    for c in range(NCH):
                        fns.append(lambda e, c=c, kt=kt, w=w, pp=pp: e.matmul(
                            pp[:, c * 512:(c + 1) * 512], w[:, kt, :],
                            S.xT[:, kt, c * 512:(c + 1) * 512],
                            start=(kt == 0), stop=(kt == G - 1)))
                ctx.op_multi("pe", fns, reads=(w_b,) + tuple(S.xT_buf),
                             writes=pbufs)
                ctx.op("act", lambda e, pp=pp, e_i=e_i, i_g=i_g: e.activation(
                    tsc[:], pp[:, 0:T], AF.Gelu,
                    bias=b1cm[:, e_i, i_g:i_g + 1]),
                    reads=pbufs + (b1cm_b,), writes=(tsc_b,))
                ctx.op("ve", lambda e, j=j, gb=gb: e.tensor_tensor(
                    hid[:, j, :], tsc[:], gb[:], ALU.mult),
                    reads=(tsc_b, gb_buf), writes=(hid_b[j],))
            first = (e_i == 0 and s == 0)
            for m in range(G):
                pp = S.pA if m % 2 == 0 else S.pB
                poff = 0 if m % 2 == 0 else 4
                pbufs = tuple(S.bank(poff + c)[1] for c in range(4))
                fns = []
                for j in range(G):
                    for c in range(NCH):
                        stop = (j == G - 1) and not first
                        fns.append(lambda e, m=m, c=c, j=j, s=s, pp=pp,
                                   stop=stop: e.matmul(
                            pp[:, c * 512:(c + 1) * 512],
                            w2e[:, s * 8 + j, m * 128:(m + 1) * 128],
                            hid[:, j, c * 512:(c + 1) * 512],
                            start=(j == 0), stop=stop))
                if first:
                    for c in range(NCH):
                        fns.append(lambda e, m=m, c=c, pp=pp: e.matmul(
                            pp[:, c * 512:(c + 1) * 512],
                            b2sb[0:8, m * 128:(m + 1) * 128],
                            g8[0:8, c * 512:(c + 1) * 512],
                            start=False, stop=True))
                ctx.op_multi("pe", fns,
                             reads=(w2e_b, b2sb_b, g8_b) + tuple(hid_b),
                             writes=pbufs)
                if first:
                    ctx.op("ve", lambda e, m=m, pp=pp: e.tensor_copy(
                        yacc[:, m, :], pp[:, 0:T]),
                        reads=pbufs, writes=(yacc_b[m],))
                else:
                    ctx.op("ve", lambda e, m=m, pp=pp: e.tensor_tensor(
                        yacc[:, m, :], pp[:, 0:T], yacc[:, m, :], ALU.add),
                        reads=pbufs, writes=(yacc_b[m],))

    for g in range(G):
        ctx.op("ve", lambda e, g=g: e.tensor_tensor(
            yacc[:, g, :], yacc[:, g, :], S.xT[:, g, :], ALU.add),
            reads=(S.xT_buf[g],), writes=(yacc_b[g],))
    emit_ln(S, yacc, yacc_b, S.xT, S.xT_buf, tsc, tsc_b)


PHASES["l1f"] = emit_l1f


def emit_identity_f32(S):
    ctx, nc = S.ctx, S.nc
    x4 = S.stack_phase.enter_context(nc.sbuf_tensor("x4", [128, G, T], F32))
    x4_buf = [Buf() for _ in range(G)]
    for g in range(G):
        ctx.op("ve", lambda e, g=g: e.tensor_copy(x4[:, g, :], S.xT[:, g, :]),
               reads=(S.xT_buf[g],), writes=(x4_buf[g],))
    return x4, x4_buf


def build_program(segs, weight_names):
    nc = bass.Bass("TRN2", target_bir_lowering=False)
    S = State()
    S.nc = nc
    S.dram = {}

    first, last = segs[0], segs[-1]
    if first == "in":
        S.dram["x"] = nc.dram_tensor("x", [T, D], F32, kind="ExternalInput").ap()
    else:
        S.dram["xT_in"] = nc.dram_tensor("xT_in", [128, G * T], BF16,
                                         kind="ExternalInput").ap()
    for w, shape in weight_names:
        S.dram[w] = nc.dram_tensor(w, list(shape), F32, kind="ExternalInput").ap()
    cdefs = host_constants()
    for name in cdefs:
        arr = cdefs[name]
        dt = F32 if arr.dtype == np.float32 else BF16
        S.dram[name] = nc.dram_tensor(name, list(arr.shape), dt,
                                      kind="ExternalInput").ap()
    if last == "out":
        S.dram["out"] = nc.dram_tensor("out", [T, D], F32, kind="ExternalOutput").ap()
    else:
        S.dram["xT_out"] = nc.dram_tensor("xT_out", [128, G * T], BF16,
                                          kind="ExternalOutput").ap()

    with ExitStack() as stack:
        ctx = Ctx(nc, stack)
        S.ctx = ctx
        alloc_psum(nc, stack, S)
        alloc_consts(nc, stack, S)
        S.xT = stack.enter_context(nc.sbuf_tensor("xT", [128, G, T], BF16))
        S.xT_buf = [Buf() for _ in range(G)]

        def run_phase(fn, name=None):
            ctx.phase = name or getattr(fn, "__name__", "phase")
            with ExitStack() as stack_phase:
                S.stack_phase = stack_phase
                fn(S)
            ctx.phase_barrier()

        if first == "in":
            run_phase(emit_load_xT_from_x)
        else:
            run_phase(emit_load_xT_handoff)

        for seg in segs:
            if seg in ("in", "out", "handoff"):
                continue
            run_phase(PHASES[seg], seg)

        if last == "out":
            ctx.phase = "outp"
            with ExitStack() as stack_phase:
                S.stack_phase = stack_phase
                x4, x4_buf = (S.final_f32 if hasattr(S, "final_f32")
                              else emit_identity_f32(S))
                emit_store_out_from_f32(S, x4, x4_buf)
        else:
            ctx.phase = "outp"
            emit_store_handoff(S)

        with nc.Block() as block:
            ctx.finalize(block)

    return nc, list(S.dram.keys())


WEIGHTS = {
    "l0a": [("L0_attn_Wq", (D, 1024)), ("L0_attn_Wk", (D, 256)),
            ("L0_attn_Wv", (D, 256)), ("L0_attn_Wo", (1024, D))],
    "l0f": [("L0_ffn_gate_w", (D, 8)), ("L0_ffn_gate_up_w", (8, D, 4096)),
            ("L0_ffn_down_w", (8, 2048, D))],
    "l1a": [("L1_attn_Wq", (D, D)), ("L1_attn_bq", (D,)),
            ("L1_attn_Wk", (D, D)), ("L1_attn_bk", (D,)),
            ("L1_attn_Wv", (D, D)), ("L1_attn_bv", (D,)),
            ("L1_attn_Wo", (D, D)), ("L1_attn_bo", (D,))],
    "l1f": [("L1_ffn_Wg", (D, 8)), ("L1_ffn_bg", (8,)),
            ("L1_ffn_W1", (8, D, 2048)), ("L1_ffn_b1", (8, 2048)),
            ("L1_ffn_W2", (8, 2048, D)), ("L1_ffn_b2", (8, D))],
}


def weights_for(segs):
    out = []
    for s in segs:
        out.extend(WEIGHTS.get(s, []))
    return out


LAUNCH_PLAN = [["in", "l0a", "l0f", "l1a", "l1f", "out"]]


def kernel(**inputs):
    consts = host_constants()
    x_full = np.ascontiguousarray(inputs["L0_attn_x"])
    n_cores = 8

    carry = [None] * n_cores
    out = None
    for li, segs in enumerate(LAUNCH_PLAN):
        wnames = weights_for(segs)
        nc, names = build_program(segs, wnames)
        in_maps = []
        for c in range(n_cores):
            m = {}
            if segs[0] == "in":
                m["x"] = np.ascontiguousarray(x_full[c])
            else:
                m["xT_in"] = carry[c]
            for w, _ in wnames:
                m[w] = np.ascontiguousarray(inputs[w])
            for k, v in consts.items():
                m[k] = v
            in_maps.append(m)
        res = run_bass_kernel_spmd(nc, in_maps, core_ids=list(range(n_cores)))
        if segs[-1] == "out":
            out = np.stack([res.results[c]["out"] for c in range(n_cores)], axis=0)
        else:
            carry = [np.ascontiguousarray(res.results[c]["xT_out"])
                     for c in range(n_cores)]
    return out
